# version 05
# speedup vs baseline: 1.2260x; 1.0007x over previous
import os
import sys
import numpy as np

sys.path.insert(0, '/opt/trn_rl_repo')
import ml_dtypes
import concourse.bass as bass
import concourse.mybir as mybir
from concourse.bass_utils import run_bass_kernel_spmd
from concourse import library_config as LC
from concourse.library_overlay import lower_extended_insts

F32 = mybir.dt.float32
BF16 = mybir.dt.bfloat16
F16 = mybir.dt.float16
I16 = mybir.dt.int16
I32 = mybir.dt.int32
U32 = mybir.dt.uint32
AF = mybir.ActivationFunctionType
ALU = mybir.AluOpType
BFNP = ml_dtypes.bfloat16

B, S, D = 8, 2048, 1024
H, HD = 16, 64
NB = S // 128
E, K, F, C = 64, 8, 768, 512
EC = 8
SF = 768
NCORE = 8
T_GLOB = B * S
TLOC = S
EPS = 1e-5

ENGINES = ("tensor", "vector", "scalar", "gpsimd", "sync")


class Sem:
    def __init__(self, handle, name):
        self.h = handle
        self.name = name
        self.n = 0

    def inc(self, k=1):
        self.n += k
        return self.n


class Prog:

    def __init__(self, nc):
        self.nc = nc
        self.streams = {e: [] for e in ENGINES}
        self._sem_stack = []

    def sem(self, name):
        h = self.nc.alloc_semaphore(name)
        return Sem(h, name)

    def on(self, eng):
        def deco(fn):
            self.streams[eng].append(fn)
            return None
        return deco

    def emit(self, eng, fn):
        self.streams[eng].append(fn)

    def finalize(self):
        nc = self.nc
        with nc.Block() as block:
            for eng in ENGINES:
                fns = self.streams[eng]
                def make(fns):
                    def run(e):
                        for f in fns:
                            f(e)
                    return run
                getattr(block, eng)(make(fns))


def ceil_div(a, b):
    return (a + b - 1) // b


class Arena:
    def __init__(self, nc, base=24576, top=229376):
        self.nc = nc
        self.base = base
        self.top = top
        self.cur = base
        self.map = {}

    def alloc(self, name, shape, dtype, at=None):
        import math
        nb = int(np.prod(shape[1:])) * mybir.dt.size(dtype)
        if at is None:
            off = (self.cur + 31) // 32 * 32
            self.cur = off + nb
            assert self.cur <= self.top, f"SBUF overflow at {name}: {self.cur}"
        else:
            off = at
            assert off + nb <= self.top, f"SBUF overflow at {name}"
        t = self.nc.alloc_sbuf_tensor_at(name, list(shape), dtype, offset=off)
        self.map[name] = (off, nb)
        return t

    def mark(self):
        return self.cur

    def reset(self, pos):
        self.cur = pos


def _dma(p, eng, sem, out, in_, **kw):
    prev = sem.n
    def f(e, out=out, in_=in_, kw=kw, sem=sem, prev=prev, eng=eng):
        if eng == "gpsimd" and prev > 0:
            e.wait_ge(sem.h, prev)
        e.dma_start(out=out, in_=in_, **kw).then_inc(sem.h, 16)
    p.emit(eng, f)
    sem.inc(16)
    return sem.n


def _wait(p, eng, sem, val=None):
    v = sem.n if val is None else val
    p.emit(eng, lambda e, s=sem.h, v=v: e.wait_ge(s, v))


def _nop(e):
    if hasattr(e, "engine_nop"):
        return e.engine_nop()
    return e.nop()


_NO_DRAIN = os.environ.get("MKND", "0") == "1"

def _op(p, eng, fn, inc=None, drain=False):
    def f(e, fn=fn, inc=inc, drain=drain):
        r = fn(e)
        if inc is not None:
            if r is not None:
                r.then_inc(inc.h, 1)
            else:
                _nop(e).then_inc(inc.h, 1)
        if drain and not _NO_DRAIN:
            e.drain()
    p.emit(eng, f)
    if inc is not None:
        inc.inc(1)
        return inc.n
    return None




def build_phase_a(stage="x1"):
    STAGE = os.environ.get("MKSTAGE", "full")
    nc = bass.Bass("TRN2", target_bir_lowering=False, debug=False, detect_race_conditions=not _NO_DRAIN)
    p = Prog(nc)

    x_my = nc.dram_tensor("x_my", [TLOC, D], F32, kind="ExternalInput").ap()
    Wmats = {w: nc.dram_tensor(w, [D, D], F32, kind="ExternalInput").ap()
             for w in ("Wq", "Wk", "Wv", "Wo")}
    bvecs = {b: nc.dram_tensor(b, [D], F32, kind="ExternalInput").ap()
             for b in ("bv",)}
    for b in ("bq", "bk", "bo"):
        bvecs[b] = nc.dram_tensor(b + "_l", [128, 8], F32, kind="ExternalInput").ap()

    out_x1 = nc.dram_tensor("out_x1", [TLOC, D], F32, kind="ExternalOutput").ap()
    out_x1n = nc.dram_tensor("out_x1n", [TLOC, D], F32, kind="ExternalOutput").ap()
    sums_dram = nc.dram_tensor("sums_dram", [16, 16, 128], F32, kind="Internal").ap()
    dbg_xn = nc.dram_tensor("dbg_xn", [128, 16, 1024], BF16, kind="ExternalOutput").ap()
    dbg_qt = nc.dram_tensor("dbg_qt", [128, 8, 2048], BF16, kind="ExternalOutput").ap()
    dbg_kt = nc.dram_tensor("dbg_kt", [128, 8, 2048], BF16, kind="ExternalOutput").ap()
    dbg_v = nc.dram_tensor("dbg_v", [128, 16, 16, 65], BF16, kind="ExternalOutput").ap()
    dbg_oraw = nc.dram_tensor("dbg_oraw", [128, 8, 2048], BF16, kind="ExternalOutput").ap()
    dbg_onorm = nc.dram_tensor("dbg_onorm", [128, 8, 2048], BF16, kind="ExternalOutput").ap()
    dbg_att = nc.dram_tensor("dbg_att", [128, 8, 2048], BF16, kind="ExternalOutput").ap()
    rp_dram = nc.dram_tensor("rp_dram", [128, 16, 16], BF16, kind="Internal")
    wo_dram = nc.dram_tensor("wo_dram", [128, 8, 1024], BF16, kind="Internal")

    ii, jj = np.meshgrid(np.arange(128), np.arange(128), indexing="ij")
    UT = (ii >= jj).astype(BFNP)
    LT = (ii <= jj).astype(BFNP)
    SELS = np.zeros((16, 8, 128), BFNP)
    for c in range(8):
        SELS[2 * c, c, 0:64] = 1
        SELS[2 * c + 1, c, 64:128] = 1
    ONES1 = np.ones((1, 128), BFNP)
    c_ut = nc.inline_tensor(np.asarray(UT), "c_ut").ap()
    c_lt = nc.inline_tensor(np.asarray(LT), "c_lt").ap()
    c_sels = nc.inline_tensor(np.asarray(SELS), "c_sels").ap()
    c_ones1 = nc.inline_tensor(np.asarray(ONES1), "c_ones1").ap()
    c_eps = nc.inline_tensor(np.full((128, 1), EPS, np.float32), "c_eps").ap()
    c_scl = nc.inline_tensor(np.full((128, 1), 0.125, np.float32), "c_scl").ap()

    BASE = 17408
    TOP = 229376
    off = [BASE]
    def A(name, shape, dtype, at=None):
        nb = int(np.prod(shape[1:])) * mybir.dt.size(dtype)
        if at is None:
            o = (off[0] + 31) // 32 * 32
            off[0] = o + nb
        else:
            o = at
        assert (o + nb) <= TOP, f"SBUF overflow {name}: {o + nb}"
        return nc.alloc_sbuf_tensor_at(name, list(shape), dtype, offset=o), o

    ut, _ = A("ut", [128, 128], BF16)
    lt, _ = A("lt", [128, 128], BF16)
    sels, _ = A("sels", [16, 8, 128], BF16)
    ones1, _ = A("ones1", [1, 128], BF16)
    bias_sb, _ = A("bias_sb", [128, 4, 8], F32)
    pass
    stats, _ = A("stats", [128, 6, 16], F32)
    eps_sb, _ = A("eps_sb", [128, 1], F32)
    scl_sb, _ = A("scl_sb", [128, 1], F32)
    dump, _ = A("dump", [128, 1024], F32)
    tile2, _ = A("tile2", [128, 2, 1024], BF16)

    Wslot0, oB0 = A("Wslot0", [128, 8, 1024], BF16)

    Wslot1, oB1 = A("Wslot1", [128, 8, 1024], BF16)
    sums64 = nc.alloc_sbuf_tensor_at("sums64", [65, 16, 128], BF16, offset=oB0)
    sumsP = nc.alloc_sbuf_tensor_at("sumsP", [128, 16, 16], F32, offset=oB0 + 4096)
    recipP = nc.alloc_sbuf_tensor_at("recipP", [128, 16, 16], F32, offset=oB0 + 5120)
    R16 = nc.alloc_sbuf_tensor_at("R16", [16, 2048], BF16, offset=oB0 + 6144)

    xs, oC = A("xs", [128, 16, 1024], F32)
    QT = nc.alloc_sbuf_tensor_at("QT", [128, 8, 2048], BF16, offset=oC)
    KT = nc.alloc_sbuf_tensor_at("KT", [128, 8, 2048], BF16, offset=oC + 32768)
    x1 = nc.alloc_sbuf_tensor_at("x1", [128, 16, 1024], F32, offset=oC)

    wstage, oD = A("wstage", [128, 8, 1024], BF16)
    off[0] = max(off[0], oD + 32768)
    xnT = nc.alloc_sbuf_tensor_at("xnT", [128, 8, 2048], BF16, offset=oD)
    Oraw = nc.alloc_sbuf_tensor_at("Oraw", [128, 8, 2048], BF16, offset=oD)
    attT = nc.alloc_sbuf_tensor_at("attT", [128, 8, 2048], BF16, offset=oD)
    x1nT = nc.alloc_sbuf_tensor_at("x1nT", [128, 8, 2048], BF16, offset=oD)
    assert oD + 32768 <= TOP

    xn_tm, oE = A("xn_tm", [128, 16, 1024], BF16)
    wbuf = nc.alloc_sbuf_tensor_at("wbuf", [128, 2, 16, 384], BF16, offset=oE)
    att_tm = nc.alloc_sbuf_tensor_at("att_tm", [128, 16, 1024], BF16, offset=oE)
    bv_row = nc.alloc_sbuf_tensor_at("bv_row", [1, 1024], BF16, offset=oE + 24576)
    tbuf = nc.alloc_sbuf_tensor_at("tbuf", [128, 8, 128], BF16, offset=oE + 26624)
    vstage = nc.alloc_sbuf_tensor_at("vstage", [128, 8, 1024], BF16, offset=oE)

    V_sb, oF = A("V_sb", [128, 16, 16, 65], BF16)
    Onorm = nc.alloc_sbuf_tensor_at("Onorm", [128, 8, 2048], BF16, offset=oF)
    xq = nc.alloc_sbuf_tensor_at("xq", [128, 4, 1024], F32, offset=oF)

    print("SBUF high-water:", off[0], "budget", TOP - BASE)

    pb = [nc.alloc_psum_tensor(f"bank{i}", [128, 512], F32) for i in range(8)]

    s_ld = p.sem("s_ld")
    s_tr = p.sem("s_tr")
    s_st = p.sem("s_st")
    s_g = p.sem("s_g")
    g_rs = p.sem("g_rs")
    t_mm = p.sem("t_mm")
    t_st = p.sem("t_st")
    t_av = p.sem("t_av")
    c_ev = p.sem("c_ev")
    e_xp = p.sem("e_xp")
    c_sm = p.sem("c_sm")
    v_ev = p.sem("v_ev")
    v_mk = p.sem("v_mk")
    v_oe = p.sem("v_oe")
    s_db = p.sem("s_db")


    def _finish():
        _wait(p, "sync", s_st, s_st.n)
        _wait(p, "sync", s_db, s_db.n)
        _wait(p, "sync", s_g, s_g.n)
        _wait(p, "sync", s_tr, s_tr.n)
        _wait(p, "sync", s_ld, s_ld.n)
        p.finalize()

    for dst, src in [(ut[:], c_ut), (lt[:], c_lt), (sels[:], c_sels),
                     (ones1[:], c_ones1)]:
        _dma(p, "sync", s_ld, dst, src)
    _dma(p, "sync", s_ld, eps_sb[:], c_eps)
    _dma(p, "sync", s_ld, scl_sb[:], c_scl)
    _dma(p, "sync", s_ld, bias_sb[:, 0, :], bvecs["bq"])
    _dma(p, "sync", s_ld, bias_sb[:, 1, :], bvecs["bk"])
    _dma(p, "sync", s_ld, bias_sb[:, 2, :], bvecs["bo"])
    consts_done = s_ld.n


    wT = {}
    for wi, (wname, slot) in enumerate([("Wq", Wslot0), ("Wk", Wslot1)]):
        if wi > 0:
            _wait(p, "gpsimd", s_tr)
        _dma(p, "gpsimd", s_g, wstage[:],
             Wmats[wname].rearrange("(co p) d -> p co d", p=128))
        _wait(p, "sync", s_g)
        for co in range(8):
            _dma(p, "sync", s_tr, slot[:, :, 128 * co:128 * (co + 1)],
                 wstage[:, co, :], transpose=True)
        wT[wname] = slot
    qk_T_done = s_tr.n
    _wait(p, "gpsimd", s_tr)
    _dma(p, "gpsimd", s_g, wstage[:], Wmats["Wo"].rearrange("(co p) d -> p co d", p=128))
    _wait(p, "sync", s_g)
    for co in range(8):
        _wait(p, "sync", s_st)
        _dma(p, "sync", s_tr, tbuf[:], wstage[:, co, :], transpose=True)
        _wait(p, "sync", s_tr)
        _dma(p, "sync", s_st, wo_dram.ap()[:, :, 128 * co:128 * (co + 1)], tbuf[:])
    wo_dram_done = s_st.n

    x_loaded = _dma(p, "sync", s_ld, xs[:], x_my.rearrange("(T p) d -> p T d", p=128))

    _wait(p, "vector", s_ld, x_loaded)

    def ln_16(src_t, apply_dst_fn):
        for T in range(16):
            _op(p, "vector", lambda e, T=T: e.reduce_sum(
                stats[:, 0, T:T + 1], src_t[:, T, :], axis=mybir.AxisListType.X))
            _op(p, "vector", lambda e, T=T: e.tensor_mul(
                dump[:], src_t[:, T, :], src_t[:, T, :]), drain=True)
            _op(p, "vector", lambda e, T=T: e.reduce_sum(
                stats[:, 1, T:T + 1], dump[:], axis=mybir.AxisListType.X), drain=True)
        _op(p, "vector", lambda e: e.drain())
        _op(p, "vector", lambda e: e.tensor_scalar_mul(stats[:, 2, :], stats[:, 0, :], 1.0 / D), drain=True)
        _op(p, "vector", lambda e: e.tensor_mul(stats[:, 3, :], stats[:, 2, :], stats[:, 2, :]), drain=True)
        vd = _op(p, "vector", lambda e: e.scalar_tensor_tensor(
            stats[:, 4, :], stats[:, 1, :], 1.0 / D, stats[:, 3, :],
            ALU.mult, ALU.subtract), inc=v_ev)
        _wait(p, "scalar", v_ev, vd)
        cd = _op(p, "scalar", lambda e: e.activation(
            out=dump[:, 0:16], in_=stats[:, 4, :], func=AF.Sqrt, bias=eps_sb[:]),
            inc=c_ev, drain=True)
        _wait(p, "vector", c_ev, cd)
        _op(p, "vector", lambda e: e.reciprocal(stats[:, 5, :], dump[:, 0:16]), drain=True)
        for T in range(16):
            apply_dst_fn(T)
        _op(p, "vector", lambda e: e.drain())
        return _op(p, "vector", lambda e: _nop(e), inc=v_ev)

    def xn_apply(T):
        _op(p, "vector", lambda e, T=T: e.tensor_scalar(
            xn_tm[:, T, :], xs[:, T, :], stats[:, 2, T:T + 1],
            stats[:, 5, T:T + 1], ALU.subtract, ALU.mult))

    _wait(p, "vector", s_st, wo_dram_done)
    xn_done = ln_16(xs, xn_apply)

    _wait(p, "sync", v_ev, xn_done)
    _wait(p, "sync", s_tr, qk_T_done)
    for T in range(16):
        _dma(p, "sync", s_tr, xnT[:, :, 128 * T:128 * (T + 1)], xn_tm[:, T, :],
             transpose=True)
    xnT_done = s_tr.n

    _dma(p, "sync", s_db, dbg_xn, xn_tm[:])
    if STAGE == "p1":
        _finish(); return nc
    _wait(p, "tensor", s_tr, xnT_done)
    _wait(p, "tensor", s_ld)
    _wait(p, "tensor", s_g)

    bank_free = {}

    def bank_wait(eng, b):
        if b in bank_free:
            sem, val = bank_free[b]
            _wait(p, eng, sem, val)

    def proj_featmaj(WTt, dstT, brow):
        for oc in range(8):
            for tq in range(4):
                b = (oc % 2) * 4 + tq
                bank_wait("tensor", b)
                for dc in range(8):
                    tv = _op(p, "tensor", lambda e, WTt=WTt, oc=oc, tq=tq, dc=dc, b=b: e.matmul(
                        pb[b][:, :], WTt[:, dc, 128 * oc:128 * (oc + 1)],
                        xnT[:, dc, 512 * tq:512 * (tq + 1)],
                        start=(dc == 0), stop=(dc == 7)),
                        inc=(t_mm if dc == 7 else None))
                _wait(p, "scalar", t_mm, tv)
                cv = _op(p, "scalar", lambda e, dstT=dstT, oc=oc, tq=tq, b=b, brow=brow: e.add(
                    dstT[:, oc, 512 * tq:512 * (tq + 1)], pb[b][:, :],
                    bias_sb[:, brow, oc:oc + 1]), inc=c_ev)
                bank_free[b] = (c_ev, cv)

    proj_featmaj(Wslot0, QT, 0)
    _wait(p, "gpsimd", s_tr, xnT_done)
    _wait(p, "gpsimd", s_db)
    _dma(p, "gpsimd", s_g, bv_row[:], bvecs["bv"].rearrange("d -> () d"))
    _dma(p, "gpsimd", s_g, vstage[:], Wmats["Wv"].rearrange("(co p) d -> p co d", p=128))
    q_done_ev = c_ev.n
    proj_featmaj(Wslot1, KT, 1)
    k_done_ev = c_ev.n

    _wait(p, "sync", c_ev, q_done_ev)
    _wait(p, "sync", s_g)
    for co in range(8):
        _dma(p, "sync", s_tr, Wslot0[:, :, 128 * co:128 * (co + 1)],
             vstage[:, co, :], transpose=True)
    wvT_done = s_tr.n

    _op(p, "vector", lambda e: e.memset(V_sb[:, :, :, 64:65], 1.0))
    vmemset = _op(p, "vector", lambda e: _nop(e), inc=v_ev)
    _wait(p, "tensor", s_tr, wvT_done)
    for tt in range(16):
        for oh in range(2):
            b = (tt % 2) * 4 + 2 + oh
            bank_wait("tensor", b)
            for dc in range(8):
                _op(p, "tensor", lambda e, tt=tt, oh=oh, dc=dc, b=b: e.matmul(
                    pb[b][:, :], xnT[:, dc, 128 * tt:128 * (tt + 1)],
                    Wslot0[:, dc, 512 * oh:512 * (oh + 1)],
                    start=(dc == 0), stop=False))
            tv = _op(p, "tensor", lambda e, oh=oh, b=b: e.matmul(
                pb[b][:, :], ones1[:, :], bv_row[:, 512 * oh:512 * (oh + 1)],
                start=False, stop=True), inc=t_mm)
            _wait(p, "vector", t_mm, tv)
            if tt == 0 and oh == 0:
                _wait(p, "vector", v_ev, vmemset)
            vv = _op(p, "vector", lambda e, tt=tt, oh=oh, b=b: e.tensor_copy(
                V_sb[:, tt, 8 * oh:8 * (oh + 1), 0:64],
                pb[b][:, :].rearrange("p (h d) -> p h d", h=8)), inc=v_ev)
            bank_free[b] = (v_ev, vv)

    _wait(p, "sync", c_ev, k_done_ev)
    _dma(p, "sync", s_ld, Wslot1[:], wo_dram.ap())
    woT_done = s_ld.n
    WoT = Wslot1

    _wait(p, "sync", c_ev, c_ev.n)
    _wait(p, "sync", v_ev, v_ev.n)
    _dma(p, "sync", s_db, dbg_qt, QT[:])
    _dma(p, "sync", s_db, dbg_kt, KT[:])
    _dma(p, "sync", s_db, dbg_v, V_sb[:])
    if STAGE == "p2":
        _finish(); return nc
    _wait(p, "tensor", c_ev, c_ev.n)
    _wait(p, "tensor", v_ev, v_ev.n)
    bank_free = {}

    exp_cnt = 0
    st_cnt = 0
    av_cnt = 0

    def valid_slots(n):
        return [j for j in range(3) if 0 <= n - 1 + j < 16]

    NLIM = {"p3a": 1, "p3b": 2, "p3c": 4, "p3d": 8, "p3e": 12, "p3f": 14, "p3g": 16}.get(STAGE, 17)
    SKIP = os.environ.get("DBG_SKIP", "")
    for n in range(NLIM):
        if n < 16:
            js = valid_slots(n)
            j0, j1 = js[0], js[-1]
            for h in range(16):
                need_exp = 16 * n + h - 1
                if need_exp > 0:
                    _wait(p, "tensor", e_xp, need_exp)
                for ji, j in enumerate(js):
                    tt = n - 1 + j
                    _op(p, "tensor", lambda e, h=h, j=j, tt=tt, n=n: e.matmul(
                        pb[h % 2][:, 128 * j:128 * (j + 1)],
                        KT[64 * (h % 2):64 * (h % 2) + 64, h // 2, 128 * tt:128 * (tt + 1)],
                        QT[64 * (h % 2):64 * (h % 2) + 64, h // 2, 128 * n:128 * (n + 1)],
                        start=True, stop=True),
                        inc=(t_st if ji == len(js) - 1 else None))
                st_cnt += 1
                _wait(p, "scalar", t_st, st_cnt)
                if n >= 2 and h == 0:
                    _wait(p, "scalar", t_av, 4 * (n - 1))
                _op(p, "scalar", lambda e, n=n, h=h, j0=j0, j1=j1: e.activation(
                    out=wbuf[:, n % 2, h, 128 * j0:128 * (j1 + 1)],
                    in_=pb[h % 2][:, 128 * j0:128 * (j1 + 1)],
                    func=AF.Exp, scale=scl_sb[:]), inc=e_xp)
                exp_cnt += 1
            _wait(p, "vector", e_xp, exp_cnt)
            if 0 in js:
                _op(p, "vector", lambda e, n=n: e.tensor_tensor(
                    wbuf[:, n % 2, :, 0:128], wbuf[:, n % 2, :, 0:128],
                    ut[:].rearrange("p q -> p () q").broadcast_to([128, 16, 128]),
                    ALU.mult))
            if 2 in js:
                _op(p, "vector", lambda e, n=n: e.tensor_tensor(
                    wbuf[:, n % 2, :, 256:384], wbuf[:, n % 2, :, 256:384],
                    lt[:].rearrange("p q -> p () q").broadcast_to([128, 16, 128]),
                    ALU.mult))
            _op(p, "vector", lambda e: e.drain())
            _op(p, "vector", lambda e: _nop(e), inc=v_mk)
        m = n - 1
        if 0 <= m < 16:
            js = valid_slots(m)
            _wait(p, "tensor", v_mk, m + 1)
            for bb in range(4):
                ob = 2 + bb
                if m >= 1:
                    _wait(p, "tensor", c_sm, 4 * (m - 1) + bb + 1)
                    _wait(p, "tensor", v_oe, 16 * (m - 1) + 4 * bb + 4)
                for hh in range(4):
                    h = 4 * bb + hh
                    ocol = 128 * hh
                    for ji, j in enumerate(js):
                        tt = m - 1 + j
                        _op(p, "tensor", lambda e, h=h, j=j, tt=tt, m=m, ob=ob, ocol=ocol,
                            first=(ji == 0), last=(ji == len(js) - 1): e.matmul(
                            pb[ob][0:65, ocol:ocol + 128],
                            V_sb[:, tt, h, 0:65],
                            wbuf[:, m % 2, h, 128 * j:128 * (j + 1)],
                            start=first, stop=last),
                            inc=(t_av if ji == len(js) - 1 and hh == 3 else None))
                av_cnt += 1
                _wait(p, "vector", t_av, av_cnt)
                if m >= 1 and bb == 0:
                    _wait(p, "vector", g_rs, 16 * m)
                _op(p, "vector", lambda e, m=m, bb=bb, ob=ob: e.tensor_copy(
                    sums64[64:65, 4 * bb:4 * (bb + 1), :], pb[ob][64:65, :]), inc=c_sm)
                for hh in range(4):
                    h = 4 * bb + hh
                    ocol = 128 * hh
                    _op(p, "vector", lambda e, m=m, h=h, ob=ob, ocol=ocol: e.tensor_copy(
                        Oraw[64 * (h % 2):64 * (h % 2) + 64, h // 2, 128 * m:128 * (m + 1)],
                        pb[ob][0:64, ocol:ocol + 128]), inc=v_oe)
            _wait(p, "gpsimd", c_sm, 4 * (m + 1))
            _dma(p, "gpsimd", g_rs, sums_dram[m], sums64[64:65, :, :])
    if STAGE in ("p3", "p3a", "p3b", "p3c", "p3d", "p3e", "p3f", "p3g"):
        _finish(); return nc
    _wait(p, "vector", s_db, s_db.n)
    _wait(p, "gpsimd", g_rs)
    _dma(p, "gpsimd", s_g, sumsP[:],
         sums_dram.rearrange("n h (qh ql) -> (h qh) n ql", ql=16))
    _wait(p, "vector", s_g, s_g.n)
    _op(p, "vector", lambda e: e.reciprocal(recipP[:], sumsP[:]), drain=True)
    rp_done = _op(p, "vector", lambda e: _nop(e), inc=v_ev)
    _wait(p, "gpsimd", v_ev, rp_done)
    _dma(p, "gpsimd", s_g, rp_dram.ap(), recipP[:])
    _wait(p, "gpsimd", s_g)
    for h in range(16):
        _dma(p, "gpsimd", s_g, R16[h:h + 1, :].rearrange("o (n q) -> o n q", q=128),
             rp_dram.ap()[8 * h:8 * (h + 1)].rearrange("qh n ql -> n qh ql"))
    r16_done = s_g.n

    _wait(p, "tensor", s_g, r16_done)
    _wait(p, "tensor", e_xp, e_xp.n)
    _wait(p, "tensor", c_sm, c_sm.n)
    _wait(p, "tensor", v_oe, v_oe.n)
    for c in range(8):
        bb = 4 * (c % 2)
        if c >= 2:
            sem, val = bank_free[("norm", c % 2)]
            _wait(p, "tensor", sem, val)
        for q4 in range(4):
            tv = _op(p, "tensor", lambda e, c=c, bb=bb, q4=q4: e.matmul(
                pb[bb + q4][:, :], sels[:, c, :],
                R16[:, 512 * q4:512 * (q4 + 1)],
                start=True, stop=True), inc=(t_mm if q4 == 3 else None))
        _wait(p, "vector", t_mm, tv)
        for q4 in range(4):
            vv = _op(p, "vector", lambda e, c=c, bb=bb, q4=q4: e.tensor_tensor(
                Onorm[:, c, 512 * q4:512 * (q4 + 1)],
                Oraw[:, c, 512 * q4:512 * (q4 + 1)],
                pb[bb + q4][:, :], ALU.mult), inc=v_ev)
        bank_free[("norm", c % 2)] = (v_ev, vv)

    _wait(p, "sync", v_oe, v_oe.n)
    _wait(p, "sync", s_db)
    _dma(p, "sync", s_db, dbg_oraw, Oraw[:])
    _wait(p, "sync", v_ev, v_ev.n)
    _wait(p, "sync", s_db)
    _dma(p, "sync", s_db, dbg_onorm, Onorm[:])
    if STAGE == "p4":
        _finish(); return nc
    _wait(p, "tensor", s_ld)
    _wait(p, "tensor", v_ev, v_ev.n)
    _wait(p, "scalar", s_db, s_db.n)
    pb_free = {}
    for oc in range(8):
        for tq in range(4):
            b = (oc % 2) * 4 + tq
            if b in pb_free:
                _wait(p, "tensor", pb_free[b][0], pb_free[b][1])
            for dc in range(8):
                tv = _op(p, "tensor", lambda e, oc=oc, tq=tq, dc=dc, b=b: e.matmul(
                    pb[b][:, :], WoT[:, dc, 128 * oc:128 * (oc + 1)],
                    Onorm[:, dc, 512 * tq:512 * (tq + 1)],
                    start=(dc == 0), stop=(dc == 7)),
                    inc=(t_mm if dc == 7 else None))
            _wait(p, "scalar", t_mm, tv)
            cv = _op(p, "scalar", lambda e, oc=oc, tq=tq, b=b: e.add(
                attT[:, oc, 512 * tq:512 * (tq + 1)], pb[b][:, :],
                bias_sb[:, 2, oc:oc + 1]), inc=c_ev)
            pb_free[b] = (c_ev, cv)
    attT_done = c_ev.n

    _wait(p, "sync", c_ev, attT_done)
    _wait(p, "sync", s_db)
    _dma(p, "sync", s_db, dbg_att, attT[:])
    for c in range(8):
        _dma(p, "sync", s_tr, att_tm[:, :, 128 * c:128 * (c + 1)], attT[:, c, :],
             transpose=True)
    atm_done = s_tr.n

    _wait(p, "vector", s_tr, atm_done)

    for T in range(16):
        _op(p, "vector", lambda e, T=T: e.reduce_sum(
            stats[:, 0, T:T + 1], att_tm[:, T, :], axis=mybir.AxisListType.X))
        _op(p, "vector", lambda e, T=T: e.tensor_mul(
            dump[:], att_tm[:, T, :], att_tm[:, T, :]), drain=True)
        _op(p, "vector", lambda e, T=T: e.reduce_sum(
            stats[:, 1, T:T + 1], dump[:], axis=mybir.AxisListType.X), drain=True)
    _op(p, "vector", lambda e: e.drain())
    _op(p, "vector", lambda e: e.tensor_scalar_mul(stats[:, 2, :], stats[:, 0, :], 1.0 / D), drain=True)
    _op(p, "vector", lambda e: e.tensor_mul(stats[:, 3, :], stats[:, 2, :], stats[:, 2, :]), drain=True)
    vd = _op(p, "vector", lambda e: e.scalar_tensor_tensor(
        stats[:, 4, :], stats[:, 1, :], 1.0 / D, stats[:, 3, :],
        ALU.mult, ALU.subtract), inc=v_ev)
    _wait(p, "scalar", v_ev, vd)
    cd = _op(p, "scalar", lambda e: e.activation(
        out=dump[:, 0:16], in_=stats[:, 4, :], func=AF.Sqrt, bias=eps_sb[:]),
        inc=c_ev, drain=True)
    _wait(p, "vector", c_ev, cd)
    _op(p, "vector", lambda e: e.reciprocal(stats[:, 5, :], dump[:, 0:16]), drain=True)

    prev_q = None
    for qd in range(4):
        _wait(p, "sync", c_ev, attT_done)
        if prev_q is not None:
            _wait(p, "sync", v_ev, prev_q)
        ld = _dma(p, "sync", s_ld, xq[:],
                  x_my.rearrange("(T p) d -> p T d", p=128)[:, 4 * qd:4 * (qd + 1), :])
        _wait(p, "vector", s_ld, ld)
        for Ti in range(4):
            T = 4 * qd + Ti
            _op(p, "vector", lambda e, T=T: e.tensor_scalar(
                dump[:], att_tm[:, T, :], stats[:, 2, T:T + 1],
                stats[:, 5, T:T + 1], ALU.subtract, ALU.mult), drain=True)
            _op(p, "vector", lambda e, T=T, Ti=Ti: e.tensor_add(
                x1[:, T, :], xq[:, Ti, :], dump[:]), drain=True)
        prev_q = _op(p, "vector", lambda e: _nop(e), inc=v_ev)
    x1_done = v_ev.n

    for T in range(16):
        _op(p, "vector", lambda e, T=T: e.reduce_sum(
            stats[:, 0, T:T + 1], x1[:, T, :], axis=mybir.AxisListType.X))
        _op(p, "vector", lambda e, T=T: e.tensor_mul(
            dump[:], x1[:, T, :], x1[:, T, :]), drain=True)
        _op(p, "vector", lambda e, T=T: e.reduce_sum(
            stats[:, 1, T:T + 1], dump[:], axis=mybir.AxisListType.X), drain=True)
    _op(p, "vector", lambda e: e.drain())
    _op(p, "vector", lambda e: e.tensor_scalar_mul(stats[:, 2, :], stats[:, 0, :], 1.0 / D), drain=True)
    _op(p, "vector", lambda e: e.tensor_mul(stats[:, 3, :], stats[:, 2, :], stats[:, 2, :]), drain=True)
    vd = _op(p, "vector", lambda e: e.scalar_tensor_tensor(
        stats[:, 4, :], stats[:, 1, :], 1.0 / D, stats[:, 3, :],
        ALU.mult, ALU.subtract), inc=v_ev)
    _wait(p, "scalar", v_ev, vd)
    cd = _op(p, "scalar", lambda e: e.activation(
        out=dump[:, 0:16], in_=stats[:, 4, :], func=AF.Sqrt, bias=eps_sb[:]),
        inc=c_ev, drain=True)
    _wait(p, "vector", c_ev, cd)
    _op(p, "vector", lambda e: e.reciprocal(stats[:, 5, :], dump[:, 0:16]), drain=True)

    _wait(p, "sync", s_db, s_db.n)
    tile_cons = {}
    for T in range(16):
        if (T % 2) in tile_cons:
            _wait(p, "vector", s_tr)
            _wait(p, "vector", s_g)
        _op(p, "vector", lambda e, T=T: e.tensor_scalar(
            tile2[:, T % 2, :], x1[:, T, :], stats[:, 2, T:T + 1],
            stats[:, 5, T:T + 1], ALU.subtract, ALU.mult), drain=True)
        tv = _op(p, "vector", lambda e: _nop(e), inc=v_ev)
        _wait(p, "sync", v_ev, tv)
        trv = _dma(p, "sync", s_tr, x1nT[:, :, 128 * T:128 * (T + 1)],
                   tile2[:, T % 2, :], transpose=True)
        _wait(p, "gpsimd", v_ev, tv)
        gv = _dma(p, "gpsimd", s_g,
                  out_x1n.rearrange("(Tp pp) d -> pp Tp d", pp=128)[:, T, :],
                  tile2[:, T % 2, :])
        tile_cons[T % 2] = (trv, gv)

    _wait(p, "sync", v_ev, x1_done)
    _dma(p, "sync", s_st, out_x1.rearrange("(T p) d -> p T d", p=128), x1[:])

    _wait(p, "sync", s_st, s_st.n)
    _wait(p, "sync", s_db, s_db.n)
    _wait(p, "sync", s_g, s_g.n)
    _wait(p, "sync", s_tr, s_tr.n)

    p.finalize()
    return nc


def build_full():
    os.environ["TRN_ALLOW_ICI_ALL_TO_ALL"] = "1"
    STAGE = os.environ.get("MKSTAGE", "full")
    nc = bass.Bass("TRN2", target_bir_lowering=False, debug=False, detect_race_conditions=not _NO_DRAIN)
    p = Prog(nc)

    x_my = nc.dram_tensor("x_my", [TLOC, D], F32, kind="ExternalInput").ap()
    Wmats = {w: nc.dram_tensor(w, [D, D], F32, kind="ExternalInput").ap()
             for w in ("Wq", "Wk", "Wv", "Wo")}
    bvecs = {"bv": nc.dram_tensor("bv", [D], F32, kind="ExternalInput").ap()}
    for b in ("bq", "bk", "bo"):
        bvecs[b] = nc.dram_tensor(b + "_l", [128, 8], F32, kind="ExternalInput").ap()
    router_w = nc.dram_tensor("router_w", [E, D], F32, kind="ExternalInput").ap()
    wg_my = nc.dram_tensor("wg_my", [EC, D, F], F32, kind="ExternalInput").ap()
    wu_my = nc.dram_tensor("wu_my", [EC, D, F], F32, kind="ExternalInput").ap()
    wd_my = nc.dram_tensor("wd_my", [EC, F, D], F32, kind="ExternalInput").ap()
    ws_gu = nc.dram_tensor("ws_gu", [D, 2 * SF], F32, kind="ExternalInput").ap()
    ws_dn = nc.dram_tensor("ws_dn", [SF, D], F32, kind="ExternalInput").ap()

    out_my = nc.dram_tensor("out_my", [TLOC, D], F32, kind="ExternalOutput").ap()

    sums_dram = nc.dram_tensor("sums_dram", [16, 16, 128], F32, kind="Internal").ap()
    rp_dram = nc.dram_tensor("rp_dram", [128, 16, 16], BF16, kind="Internal")
    wo_dram = nc.dram_tensor("wo_dram", [128, 8, 1024], BF16, kind="Internal")
    x1_hbm = nc.dram_tensor("x1_hbm", [128, 16, 1024], F32, kind="Internal").ap()
    x1n_my = nc.dram_tensor("x1n_my", [TLOC, D], BF16, kind="Internal").ap()
    x1n_all = nc.dram_tensor("x1n_all", [T_GLOB, D], BF16, kind="Internal", addr_space="Shared").ap()
    a1_in = nc.dram_tensor("a1_in", [64, TLOC], F16, kind="Internal").ap()
    a1_out = nc.dram_tensor("a1_out", [64, TLOC], F16, kind="Internal").ap()
    a2_in = nc.dram_tensor("a2_in", [64, TLOC], F16, kind="Internal").ap()
    a2_out = nc.dram_tensor("a2_out", [64, TLOC], F16, kind="Internal").ap()
    y_my = nc.dram_tensor("y_my", [EC * C, D], BF16, kind="Internal").ap()
    y_all = nc.dram_tensor("y_all", [E * C, D], BF16, kind="Internal", addr_space="Shared").ap()
    tokm_dram = nc.dram_tensor("tokm_dram", [128, 512], I16, kind="Internal").ap()
    gatem_dram = nc.dram_tensor("gatem_dram", [128, 512], F16, kind="Internal").ap()
    cnt_dram = nc.dram_tensor("cnt_dram", [128, 1], F32, kind="Internal").ap()
    tok8_dram = nc.dram_tensor("tok8_dram", [8, 512], I16, kind="Internal").ap()
    gate8_dram = nc.dram_tensor("gate8_dram", [8, 512], F16, kind="Internal").ap()
    ridx_dram = nc.dram_tensor("ridx_dram", [128, 8, 16], I16, kind="Internal").ap()
    ninv_dram = nc.dram_tensor("ninv_dram", [128, 16], BF16, kind="Internal").ap()

    ii, jj = np.meshgrid(np.arange(128), np.arange(128), indexing="ij")
    UT = (ii >= jj).astype(BFNP)
    LT = (ii <= jj).astype(BFNP)
    SELS = np.zeros((16, 8, 128), BFNP)
    for c in range(8):
        SELS[2 * c, c, 0:64] = 1
        SELS[2 * c + 1, c, 64:128] = 1
    ONES1 = np.ones((1, 128), BFNP)
    c_ut = nc.inline_tensor(np.asarray(UT), "c_ut").ap()
    c_lt = nc.inline_tensor(np.asarray(LT), "c_lt").ap()
    c_sels = nc.inline_tensor(np.asarray(SELS), "c_sels").ap()
    c_ones1 = nc.inline_tensor(np.asarray(ONES1), "c_ones1").ap()
    c_eps = nc.inline_tensor(np.full((128, 1), EPS, np.float32), "c_eps").ap()
    c_scl = nc.inline_tensor(np.full((128, 1), 0.125, np.float32), "c_scl").ap()
    TOKP1 = np.zeros((128, 1024), np.int16)
    for r in range(128):
        b = r // 16
        half = r % 2
        base = b * 2048 + half * 1024
        TOKP1[r, :] = np.arange(base + 1, base + 1025, dtype=np.int16)
    offv = np.zeros((128, 64), np.float32)
    for e_ in range(64):
        offv[:, e_] = 256 * e_ + 256 * (e_ % 8) + (14336 if (e_ % 8) >= 4 else 0)
    c_off64 = nc.inline_tensor(offv, "c_off64").ap()
    c_tokp1 = nc.inline_tensor(TOKP1, "c_tokp1").ap()
    E64 = np.tile(np.arange(64, dtype=np.float32), (128, 1))
    c_e64 = nc.inline_tensor(E64, "c_e64").ap()
    OFFT = np.zeros((128, 16, 8), np.float32)
    for T in range(16):
        OFFT[:, T, :] = 128 * T
    c_offt = nc.inline_tensor(OFFT.reshape(128, 128), "c_offt").ap()

    BASE = 17408
    TOP = 229376
    off = [BASE]

    def A(name, shape, dtype, at=None):
        nb = int(np.prod(shape[1:])) * mybir.dt.size(dtype)
        if at is None:
            o = (off[0] + 31) // 32 * 32
            off[0] = o + nb
        else:
            o = at
        assert (o + nb) <= TOP, f"SBUF overflow {name}: {o + nb}"
        return nc.alloc_sbuf_tensor_at(name, list(shape), dtype, offset=o), o

    ut, _ = A("ut", [128, 128], BF16)
    lt, _ = A("lt", [128, 128], BF16)
    sels, _ = A("sels", [16, 8, 128], BF16)
    ones1, _ = A("ones1", [1, 128], BF16)
    eps_sb, _ = A("eps_sb", [128, 1], F32)
    scl_sb, _ = A("scl_sb", [128, 1], F32)
    bias_sb, _ = A("bias_sb", [128, 4, 8], F32)
    stats, _ = A("stats", [128, 6, 16], F32)
    dump, _ = A("dump", [128, 1024], F32)
    tile2, _ = A("tile2", [128, 2, 1024], BF16)

    Wslot0, oB0 = A("Wslot0", [128, 8, 1024], BF16)
    Wslot1, oB1 = A("Wslot1", [128, 8, 1024], BF16)
    sums64 = nc.alloc_sbuf_tensor_at("sums64", [65, 16, 128], BF16, offset=oB0)
    sumsP = nc.alloc_sbuf_tensor_at("sumsP", [128, 16, 16], F32, offset=oB0 + 4096)
    recipP = nc.alloc_sbuf_tensor_at("recipP", [128, 16, 16], F32, offset=oB0 + 5120)
    R16 = nc.alloc_sbuf_tensor_at("R16", [16, 2048], BF16, offset=oB0 + 6144)

    xs, oC = A("xs", [128, 16, 1024], F32)
    QT = nc.alloc_sbuf_tensor_at("QT", [128, 8, 2048], BF16, offset=oC)
    KT = nc.alloc_sbuf_tensor_at("KT", [128, 8, 2048], BF16, offset=oC + 32768)
    x1 = nc.alloc_sbuf_tensor_at("x1", [128, 16, 1024], F32, offset=oC)
    wslots = [nc.alloc_sbuf_tensor_at(f"ew{i}", [128, 8, 768], BF16,
                                      offset=oC + 12288 * i) for i in range(5)]

    wstage, oD = A("wstage", [128, 8, 1024], BF16)
    bv_row = None
    off[0] = max(off[0], oD + 32768)
    xnT = nc.alloc_sbuf_tensor_at("xnT", [128, 8, 2048], BF16, offset=oD)
    Oraw = nc.alloc_sbuf_tensor_at("Oraw", [128, 8, 2048], BF16, offset=oD)
    attT = nc.alloc_sbuf_tensor_at("attT", [128, 8, 2048], BF16, offset=oD)
    x1nT = nc.alloc_sbuf_tensor_at("x1nT", [128, 8, 2048], BF16, offset=oD)

    xn_tm, oE = A("xn_tm", [128, 16, 1024], BF16)
    wbuf = nc.alloc_sbuf_tensor_at("wbuf", [128, 2, 16, 384], BF16, offset=oE)
    att_tm = nc.alloc_sbuf_tensor_at("att_tm", [128, 16, 1024], BF16, offset=oE)
    bv_row = nc.alloc_sbuf_tensor_at("bv_row", [1, 1024], BF16, offset=oE + 24576)
    tbuf = nc.alloc_sbuf_tensor_at("tbuf", [128, 8, 128], BF16, offset=oE + 26624)
    vstage = nc.alloc_sbuf_tensor_at("vstage", [128, 8, 1024], BF16, offset=oE)
    Xe = nc.alloc_sbuf_tensor_at("Xe", [128, 2, 8, 512], BF16, offset=oE)
    hbv = nc.alloc_sbuf_tensor_at("hbv", [128, 6, 512], BF16, offset=oE + 16384)
    yT = nc.alloc_sbuf_tensor_at("yT", [128, 8, 512], BF16, offset=oE + 22528)

    V_sb, oF = A("V_sb", [128, 16, 16, 65], BF16)
    Onorm = nc.alloc_sbuf_tensor_at("Onorm", [128, 8, 2048], BF16, offset=oF)
    xq = nc.alloc_sbuf_tensor_at("xq", [128, 4, 1024], F32, offset=oF)
    ytm = nc.alloc_sbuf_tensor_at("ytm", [128, 4, 1024], BF16, offset=oF)
    logits = nc.alloc_sbuf_tensor_at("logits", [128, 16, 64], F32, offset=oF + 8192)
    max8 = nc.alloc_sbuf_tensor_at("max8", [128, 16, 8], F32, offset=oF + 12288)
    w8 = nc.alloc_sbuf_tensor_at("w8", [128, 16, 8], F32, offset=oF + 12800)
    idsf = nc.alloc_sbuf_tensor_at("idsf", [128, 16, 8], F32, offset=oF + 13312)
    ids_u = nc.alloc_sbuf_tensor_at("ids_u", [128, 16, 8], U32, offset=oF + 13824)
    sm1 = nc.alloc_sbuf_tensor_at("sm1", [128, 16], F32, offset=oF + 14336)
    sm2 = nc.alloc_sbuf_tensor_at("sm2", [128, 16], F32, offset=oF + 14464)
    wloc = nc.alloc_sbuf_tensor_at("wloc", [128, 16, 128], F16, offset=oF + 14592)
    wlocT = nc.alloc_sbuf_tensor_at("wlocT", [128, 16, 128], F16, offset=oF + 18688)
    posk = nc.alloc_sbuf_tensor_at("posk", [128, 16, 8], F32, offset=oF + 22784)
    maskk = nc.alloc_sbuf_tensor_at("maskk", [128, 16, 8], F32, offset=oF + 23296)
    ridx = nc.alloc_sbuf_tensor_at("ridx", [128, 8, 16], I16, offset=oF + 23808)
    pft = nc.alloc_sbuf_tensor_at("pft", [128, 16, 128], F16, offset=oF + 24096)
    scr1 = nc.alloc_sbuf_tensor_at("scr1", [128, 16, 8], F32, offset=oF + 28192)
    scr2 = nc.alloc_sbuf_tensor_at("scr2", [128, 16, 8], F32, offset=oF + 28704)
    i32a = nc.alloc_sbuf_tensor_at("i32a", [128, 16, 8], I32, offset=oF + 29216)
    rowk = nc.alloc_sbuf_tensor_at("rowk", [128, 16, 8], F32, offset=oF + 29728)

    ob = oB0
    wraw = nc.alloc_sbuf_tensor_at("wraw", [128, 1024], F16, offset=ob)
    obf = nc.alloc_sbuf_tensor_at("obf", [128, 1024], F32, offset=ob + 2048)
    scn = nc.alloc_sbuf_tensor_at("scn", [128, 1024], F32, offset=ob + 6144)
    posb = nc.alloc_sbuf_tensor_at("posb", [128, 1024], F32, offset=ob + 10240)
    vvv = nc.alloc_sbuf_tensor_at("vvv", [128, 1024], F32, offset=ob + 14336)
    sidx = nc.alloc_sbuf_tensor_at("sidx", [128, 1024], I16, offset=ob + 18432)
    sdat = nc.alloc_sbuf_tensor_at("sdat", [128, 1024], I16, offset=ob + 20480)
    sgat = nc.alloc_sbuf_tensor_at("sgat", [128, 1024], F16, offset=ob + 22528)
    tot64 = nc.alloc_sbuf_tensor_at("tot64", [128, 1], F32, offset=ob + 24576)
    tb8 = nc.alloc_sbuf_tensor_at("tb8", [8, 16], F32, offset=ob + 24640)
    tb8s = nc.alloc_sbuf_tensor_at("tb8s", [8, 16], F32, offset=ob + 24704)
    tb8z = nc.alloc_sbuf_tensor_at("tb8z", [8, 16], F32, offset=ob + 24768)
    grow = nc.alloc_sbuf_tensor_at("grow", [1, 4096], F16, offset=ob + 2048)
    ninv_row = nc.alloc_sbuf_tensor_at("ninv", [1, 2048], BF16, offset=ob + 10240)

    tokp1c = nc.alloc_sbuf_tensor_at("tokp1c", [128, 1024], I16, offset=ob + 24832)
    e64c = nc.alloc_sbuf_tensor_at("e64c", [128, 64], F32, offset=ob + 26880)
    offtc = nc.alloc_sbuf_tensor_at("offtc", [128, 128], F32, offset=ob + 27136)

    tokm = nc.alloc_sbuf_tensor_at("tokm", [128, 512], I16, offset=oE)
    gatem = nc.alloc_sbuf_tensor_at("gatem", [128, 512], F16, offset=oE + 1024)
    mrg = nc.alloc_sbuf_tensor_at("mrg", [8, 16, 512], I16, offset=oE + 2048)
    mrgg = nc.alloc_sbuf_tensor_at("mrgg", [8, 16, 512], F16, offset=oE + 2048)

    tok8, _ = A("tok8", [8, 512], I16)
    gate8, _ = A("gate8", [8, 512], F16)
    idx16, _ = A("idx16", [128, 256], I16)
    cnt8, _ = A("cnt8", [1, 8], F32)
    cnt8i, _ = A("cnt8i", [1, 8], I32)
    c512i, _ = A("c512i", [1, 1], I32)
    idx8 = nc.alloc_sbuf_tensor_at("idx8", [128, 8, 128], I16, offset=ob + 27648)
    offc = nc.alloc_sbuf_tensor_at("offc", [128, 64], F32, offset=ob + 29696)

    Wb = nc.alloc_sbuf_tensor_at("Wb", [128, 512], F32, offset=oF + 29728)
    y0row = nc.alloc_sbuf_tensor_at("y0row", [1, 1024], BF16, offset=oF + 31776)

    print("SBUF high-water:", off[0], "limit", TOP)
    assert off[0] <= TOP

    routedT = nc.alloc_sbuf_tensor_at("routedT", [128, 8, 2048], F32, offset=oC)
    Rk = nc.alloc_sbuf_tensor_at("Rk", [128, 8, 2048], BF16, offset=oE)
    Rk4 = nc.alloc_sbuf_tensor_at("Rk4", [128, 4, 8, 512], BF16, offset=oE)
    hsh = nc.alloc_sbuf_tensor_at("hsh", [128, 6, 2048], BF16, offset=oF)
    ffnT = nc.alloc_sbuf_tensor_at("ffnT", [128, 8, 2048], BF16, offset=oD)
    ffn_tm = nc.alloc_sbuf_tensor_at("ffn_tm", [128, 16, 1024], BF16, offset=oF)
    x1q = nc.alloc_sbuf_tensor_at("x1q", [128, 4, 1024], F32, offset=oC)


    pb = [nc.alloc_psum_tensor(f"bank{i}", [128, 512], F32) for i in range(8)]
    s_ld = p.sem("s_ld")
    s_tr = p.sem("s_tr")
    s_st = p.sem("s_st")
    s_g = p.sem("s_g")
    g_rs = p.sem("g_rs")
    t_mm = p.sem("t_mm")
    t_st = p.sem("t_st")
    t_av = p.sem("t_av")
    c_ev = p.sem("c_ev")
    e_xp = p.sem("e_xp")
    c_sm = p.sem("c_sm")
    v_ev = p.sem("v_ev")
    v_mk = p.sem("v_mk")
    v_oe = p.sem("v_oe")

    def _finish():
        _wait(p, "sync", s_st, s_st.n)
        _wait(p, "sync", s_g, s_g.n)
        _wait(p, "sync", s_tr, s_tr.n)
        _wait(p, "sync", s_ld, s_ld.n)
        p.finalize()

    for dst, src in [(ut[:], c_ut), (lt[:], c_lt), (sels[:], c_sels),
                     (ones1[:], c_ones1)]:
        _dma(p, "sync", s_ld, dst, src)
    _dma(p, "sync", s_ld, eps_sb[:], c_eps)
    _dma(p, "sync", s_ld, scl_sb[:], c_scl)
    _dma(p, "sync", s_ld, bias_sb[:, 0, :], bvecs["bq"])
    _dma(p, "sync", s_ld, bias_sb[:, 1, :], bvecs["bk"])
    _dma(p, "sync", s_ld, bias_sb[:, 2, :], bvecs["bo"])
    consts_done = s_ld.n


    wT = {}
    for wi, (wname, slot) in enumerate([("Wq", Wslot0), ("Wk", Wslot1)]):
        if wi > 0:
            _wait(p, "gpsimd", s_tr)
        _dma(p, "gpsimd", s_g, wstage[:],
             Wmats[wname].rearrange("(co p) d -> p co d", p=128))
        _wait(p, "sync", s_g)
        for co in range(8):
            _dma(p, "sync", s_tr, slot[:, :, 128 * co:128 * (co + 1)],
                 wstage[:, co, :], transpose=True)
        wT[wname] = slot
    qk_T_done = s_tr.n
    _wait(p, "gpsimd", s_tr)
    _dma(p, "gpsimd", s_g, wstage[:], Wmats["Wo"].rearrange("(co p) d -> p co d", p=128))
    _wait(p, "sync", s_g)
    for co in range(8):
        _wait(p, "sync", s_st)
        _dma(p, "sync", s_tr, tbuf[:], wstage[:, co, :], transpose=True)
        _wait(p, "sync", s_tr)
        _dma(p, "sync", s_st, wo_dram.ap()[:, :, 128 * co:128 * (co + 1)], tbuf[:])
    wo_dram_done = s_st.n

    x_loaded = _dma(p, "sync", s_ld, xs[:], x_my.rearrange("(T p) d -> p T d", p=128))

    _wait(p, "vector", s_ld, x_loaded)

    def ln_16(src_t, apply_dst_fn):
        for T in range(16):
            _op(p, "vector", lambda e, T=T: e.reduce_sum(
                stats[:, 0, T:T + 1], src_t[:, T, :], axis=mybir.AxisListType.X))
            _op(p, "vector", lambda e, T=T: e.tensor_mul(
                dump[:], src_t[:, T, :], src_t[:, T, :]), drain=True)
            _op(p, "vector", lambda e, T=T: e.reduce_sum(
                stats[:, 1, T:T + 1], dump[:], axis=mybir.AxisListType.X), drain=True)
        _op(p, "vector", lambda e: e.drain())
        _op(p, "vector", lambda e: e.tensor_scalar_mul(stats[:, 2, :], stats[:, 0, :], 1.0 / D), drain=True)
        _op(p, "vector", lambda e: e.tensor_mul(stats[:, 3, :], stats[:, 2, :], stats[:, 2, :]), drain=True)
        vd = _op(p, "vector", lambda e: e.scalar_tensor_tensor(
            stats[:, 4, :], stats[:, 1, :], 1.0 / D, stats[:, 3, :],
            ALU.mult, ALU.subtract), inc=v_ev)
        _wait(p, "scalar", v_ev, vd)
        cd = _op(p, "scalar", lambda e: e.activation(
            out=dump[:, 0:16], in_=stats[:, 4, :], func=AF.Sqrt, bias=eps_sb[:]),
            inc=c_ev, drain=True)
        _wait(p, "vector", c_ev, cd)
        _op(p, "vector", lambda e: e.reciprocal(stats[:, 5, :], dump[:, 0:16]), drain=True)
        for T in range(16):
            apply_dst_fn(T)
        _op(p, "vector", lambda e: e.drain())
        return _op(p, "vector", lambda e: _nop(e), inc=v_ev)

    def xn_apply(T):
        _op(p, "vector", lambda e, T=T: e.tensor_scalar(
            xn_tm[:, T, :], xs[:, T, :], stats[:, 2, T:T + 1],
            stats[:, 5, T:T + 1], ALU.subtract, ALU.mult))

    _wait(p, "vector", s_st, wo_dram_done)
    xn_done = ln_16(xs, xn_apply)

    _wait(p, "sync", v_ev, xn_done)
    _wait(p, "sync", s_tr, qk_T_done)
    for T in range(16):
        _dma(p, "sync", s_tr, xnT[:, :, 128 * T:128 * (T + 1)], xn_tm[:, T, :],
             transpose=True)
    xnT_done = s_tr.n

    _wait(p, "tensor", s_tr, xnT_done)
    _wait(p, "tensor", s_ld)
    _wait(p, "tensor", s_g)

    bank_free = {}

    def bank_wait(eng, b):
        if b in bank_free:
            sem, val = bank_free[b]
            _wait(p, eng, sem, val)

    def proj_featmaj(WTt, dstT, brow):
        for oc in range(8):
            for tq in range(4):
                b = (oc % 2) * 4 + tq
                bank_wait("tensor", b)
                for dc in range(8):
                    tv = _op(p, "tensor", lambda e, WTt=WTt, oc=oc, tq=tq, dc=dc, b=b: e.matmul(
                        pb[b][:, :], WTt[:, dc, 128 * oc:128 * (oc + 1)],
                        xnT[:, dc, 512 * tq:512 * (tq + 1)],
                        start=(dc == 0), stop=(dc == 7)),
                        inc=(t_mm if dc == 7 else None))
                _wait(p, "scalar", t_mm, tv)
                cv = _op(p, "scalar", lambda e, dstT=dstT, oc=oc, tq=tq, b=b, brow=brow: e.add(
                    dstT[:, oc, 512 * tq:512 * (tq + 1)], pb[b][:, :],
                    bias_sb[:, brow, oc:oc + 1]), inc=c_ev)
                bank_free[b] = (c_ev, cv)

    proj_featmaj(Wslot0, QT, 0)
    _wait(p, "gpsimd", s_tr, xnT_done)
    _dma(p, "gpsimd", s_g, bv_row[:], bvecs["bv"].rearrange("d -> () d"))
    _dma(p, "gpsimd", s_g, vstage[:], Wmats["Wv"].rearrange("(co p) d -> p co d", p=128))
    q_done_ev = c_ev.n
    proj_featmaj(Wslot1, KT, 1)
    k_done_ev = c_ev.n

    _wait(p, "sync", c_ev, q_done_ev)
    _wait(p, "sync", s_g)
    for co in range(8):
        _dma(p, "sync", s_tr, Wslot0[:, :, 128 * co:128 * (co + 1)],
             vstage[:, co, :], transpose=True)
    wvT_done = s_tr.n

    _op(p, "vector", lambda e: e.memset(V_sb[:, :, :, 64:65], 1.0))
    vmemset = _op(p, "vector", lambda e: _nop(e), inc=v_ev)
    _wait(p, "tensor", s_tr, wvT_done)
    for tt in range(16):
        for oh in range(2):
            b = (tt % 2) * 4 + 2 + oh
            bank_wait("tensor", b)
            for dc in range(8):
                _op(p, "tensor", lambda e, tt=tt, oh=oh, dc=dc, b=b: e.matmul(
                    pb[b][:, :], xnT[:, dc, 128 * tt:128 * (tt + 1)],
                    Wslot0[:, dc, 512 * oh:512 * (oh + 1)],
                    start=(dc == 0), stop=False))
            tv = _op(p, "tensor", lambda e, oh=oh, b=b: e.matmul(
                pb[b][:, :], ones1[:, :], bv_row[:, 512 * oh:512 * (oh + 1)],
                start=False, stop=True), inc=t_mm)
            _wait(p, "vector", t_mm, tv)
            if tt == 0 and oh == 0:
                _wait(p, "vector", v_ev, vmemset)
            vv = _op(p, "vector", lambda e, tt=tt, oh=oh, b=b: e.tensor_copy(
                V_sb[:, tt, 8 * oh:8 * (oh + 1), 0:64],
                pb[b][:, :].rearrange("p (h d) -> p h d", h=8)), inc=v_ev)
            bank_free[b] = (v_ev, vv)

    _wait(p, "sync", c_ev, k_done_ev)
    _dma(p, "sync", s_ld, Wslot1[:], wo_dram.ap())
    woT_done = s_ld.n
    WoT = Wslot1

    _wait(p, "sync", c_ev, c_ev.n)
    _wait(p, "sync", v_ev, v_ev.n)
    _wait(p, "tensor", c_ev, c_ev.n)
    _wait(p, "tensor", v_ev, v_ev.n)
    bank_free = {}

    exp_cnt = 0
    st_cnt = 0
    av_cnt = 0

    def valid_slots(n):
        return [j for j in range(3) if 0 <= n - 1 + j < 16]

    NLIM = 17
    SKIP = ""
    for n in range(NLIM):
        if n < 16:
            js = valid_slots(n)
            j0, j1 = js[0], js[-1]
            for h in range(16):
                need_exp = 16 * n + h - 1
                if need_exp > 0:
                    _wait(p, "tensor", e_xp, need_exp)
                for ji, j in enumerate(js):
                    tt = n - 1 + j
                    _op(p, "tensor", lambda e, h=h, j=j, tt=tt, n=n: e.matmul(
                        pb[h % 2][:, 128 * j:128 * (j + 1)],
                        KT[64 * (h % 2):64 * (h % 2) + 64, h // 2, 128 * tt:128 * (tt + 1)],
                        QT[64 * (h % 2):64 * (h % 2) + 64, h // 2, 128 * n:128 * (n + 1)],
                        start=True, stop=True),
                        inc=(t_st if ji == len(js) - 1 else None))
                st_cnt += 1
                _wait(p, "scalar", t_st, st_cnt)
                if n >= 2 and h == 0:
                    _wait(p, "scalar", t_av, 4 * (n - 1))
                _op(p, "scalar", lambda e, n=n, h=h, j0=j0, j1=j1: e.activation(
                    out=wbuf[:, n % 2, h, 128 * j0:128 * (j1 + 1)],
                    in_=pb[h % 2][:, 128 * j0:128 * (j1 + 1)],
                    func=AF.Exp, scale=scl_sb[:]), inc=e_xp)
                exp_cnt += 1
            _wait(p, "vector", e_xp, exp_cnt)
            if 0 in js:
                _op(p, "vector", lambda e, n=n: e.tensor_tensor(
                    wbuf[:, n % 2, :, 0:128], wbuf[:, n % 2, :, 0:128],
                    ut[:].rearrange("p q -> p () q").broadcast_to([128, 16, 128]),
                    ALU.mult))
            if 2 in js:
                _op(p, "vector", lambda e, n=n: e.tensor_tensor(
                    wbuf[:, n % 2, :, 256:384], wbuf[:, n % 2, :, 256:384],
                    lt[:].rearrange("p q -> p () q").broadcast_to([128, 16, 128]),
                    ALU.mult))
            _op(p, "vector", lambda e: e.drain())
            _op(p, "vector", lambda e: _nop(e), inc=v_mk)
        m = n - 1
        if 0 <= m < 16:
            js = valid_slots(m)
            _wait(p, "tensor", v_mk, m + 1)
            for bb in range(4):
                ob = 2 + bb
                if m >= 1:
                    _wait(p, "tensor", c_sm, 4 * (m - 1) + bb + 1)
                    _wait(p, "tensor", v_oe, 16 * (m - 1) + 4 * bb + 4)
                for hh in range(4):
                    h = 4 * bb + hh
                    ocol = 128 * hh
                    for ji, j in enumerate(js):
                        tt = m - 1 + j
                        _op(p, "tensor", lambda e, h=h, j=j, tt=tt, m=m, ob=ob, ocol=ocol,
                            first=(ji == 0), last=(ji == len(js) - 1): e.matmul(
                            pb[ob][0:65, ocol:ocol + 128],
                            V_sb[:, tt, h, 0:65],
                            wbuf[:, m % 2, h, 128 * j:128 * (j + 1)],
                            start=first, stop=last),
                            inc=(t_av if ji == len(js) - 1 and hh == 3 else None))
                av_cnt += 1
                _wait(p, "vector", t_av, av_cnt)
                if m >= 1 and bb == 0:
                    _wait(p, "vector", g_rs, 16 * m)
                _op(p, "vector", lambda e, m=m, bb=bb, ob=ob: e.tensor_copy(
                    sums64[64:65, 4 * bb:4 * (bb + 1), :], pb[ob][64:65, :]), inc=c_sm)
                for hh in range(4):
                    h = 4 * bb + hh
                    ocol = 128 * hh
                    _op(p, "vector", lambda e, m=m, h=h, ob=ob, ocol=ocol: e.tensor_copy(
                        Oraw[64 * (h % 2):64 * (h % 2) + 64, h // 2, 128 * m:128 * (m + 1)],
                        pb[ob][0:64, ocol:ocol + 128]), inc=v_oe)
            _wait(p, "gpsimd", c_sm, 4 * (m + 1))
            _dma(p, "gpsimd", g_rs, sums_dram[m], sums64[64:65, :, :])
    _wait(p, "gpsimd", g_rs)
    _dma(p, "gpsimd", s_g, sumsP[:],
         sums_dram.rearrange("n h (qh ql) -> (h qh) n ql", ql=16))
    _wait(p, "vector", s_g, s_g.n)
    _op(p, "vector", lambda e: e.reciprocal(recipP[:], sumsP[:]), drain=True)
    rp_done = _op(p, "vector", lambda e: _nop(e), inc=v_ev)
    _wait(p, "gpsimd", v_ev, rp_done)
    _dma(p, "gpsimd", s_g, rp_dram.ap(), recipP[:])
    _wait(p, "gpsimd", s_g)
    for h in range(16):
        _dma(p, "gpsimd", s_g, R16[h:h + 1, :].rearrange("o (n q) -> o n q", q=128),
             rp_dram.ap()[8 * h:8 * (h + 1)].rearrange("qh n ql -> n qh ql"))
    r16_done = s_g.n

    _wait(p, "tensor", s_g, r16_done)
    _wait(p, "tensor", e_xp, e_xp.n)
    _wait(p, "tensor", c_sm, c_sm.n)
    _wait(p, "tensor", v_oe, v_oe.n)
    for c in range(8):
        bb = 4 * (c % 2)
        if c >= 2:
            sem, val = bank_free[("norm", c % 2)]
            _wait(p, "tensor", sem, val)
        for q4 in range(4):
            tv = _op(p, "tensor", lambda e, c=c, bb=bb, q4=q4: e.matmul(
                pb[bb + q4][:, :], sels[:, c, :],
                R16[:, 512 * q4:512 * (q4 + 1)],
                start=True, stop=True), inc=(t_mm if q4 == 3 else None))
        _wait(p, "vector", t_mm, tv)
        for q4 in range(4):
            vv = _op(p, "vector", lambda e, c=c, bb=bb, q4=q4: e.tensor_tensor(
                Onorm[:, c, 512 * q4:512 * (q4 + 1)],
                Oraw[:, c, 512 * q4:512 * (q4 + 1)],
                pb[bb + q4][:, :], ALU.mult), inc=v_ev)
        bank_free[("norm", c % 2)] = (v_ev, vv)

    _wait(p, "sync", v_oe, v_oe.n)
    _wait(p, "sync", v_ev, v_ev.n)
    _wait(p, "tensor", s_ld)
    _wait(p, "tensor", v_ev, v_ev.n)
    pb_free = {}
    for oc in range(8):
        for tq in range(4):
            b = (oc % 2) * 4 + tq
            if b in pb_free:
                _wait(p, "tensor", pb_free[b][0], pb_free[b][1])
            for dc in range(8):
                tv = _op(p, "tensor", lambda e, oc=oc, tq=tq, dc=dc, b=b: e.matmul(
                    pb[b][:, :], WoT[:, dc, 128 * oc:128 * (oc + 1)],
                    Onorm[:, dc, 512 * tq:512 * (tq + 1)],
                    start=(dc == 0), stop=(dc == 7)),
                    inc=(t_mm if dc == 7 else None))
            _wait(p, "scalar", t_mm, tv)
            cv = _op(p, "scalar", lambda e, oc=oc, tq=tq, b=b: e.add(
                attT[:, oc, 512 * tq:512 * (tq + 1)], pb[b][:, :],
                bias_sb[:, 2, oc:oc + 1]), inc=c_ev)
            pb_free[b] = (c_ev, cv)
    attT_done = c_ev.n

    _wait(p, "sync", c_ev, attT_done)
    for c in range(8):
        _dma(p, "sync", s_tr, att_tm[:, :, 128 * c:128 * (c + 1)], attT[:, c, :],
             transpose=True)
    atm_done = s_tr.n

    _wait(p, "vector", s_tr, atm_done)

    for T in range(16):
        _op(p, "vector", lambda e, T=T: e.reduce_sum(
            stats[:, 0, T:T + 1], att_tm[:, T, :], axis=mybir.AxisListType.X))
        _op(p, "vector", lambda e, T=T: e.tensor_mul(
            dump[:], att_tm[:, T, :], att_tm[:, T, :]), drain=True)
        _op(p, "vector", lambda e, T=T: e.reduce_sum(
            stats[:, 1, T:T + 1], dump[:], axis=mybir.AxisListType.X), drain=True)
    _op(p, "vector", lambda e: e.drain())
    _op(p, "vector", lambda e: e.tensor_scalar_mul(stats[:, 2, :], stats[:, 0, :], 1.0 / D), drain=True)
    _op(p, "vector", lambda e: e.tensor_mul(stats[:, 3, :], stats[:, 2, :], stats[:, 2, :]), drain=True)
    vd = _op(p, "vector", lambda e: e.scalar_tensor_tensor(
        stats[:, 4, :], stats[:, 1, :], 1.0 / D, stats[:, 3, :],
        ALU.mult, ALU.subtract), inc=v_ev)
    _wait(p, "scalar", v_ev, vd)
    cd = _op(p, "scalar", lambda e: e.activation(
        out=dump[:, 0:16], in_=stats[:, 4, :], func=AF.Sqrt, bias=eps_sb[:]),
        inc=c_ev, drain=True)
    _wait(p, "vector", c_ev, cd)
    _op(p, "vector", lambda e: e.reciprocal(stats[:, 5, :], dump[:, 0:16]), drain=True)

    prev_q = None
    for qd in range(4):
        _wait(p, "sync", c_ev, attT_done)
        if prev_q is not None:
            _wait(p, "sync", v_ev, prev_q)
        ld = _dma(p, "sync", s_ld, xq[:],
                  x_my.rearrange("(T p) d -> p T d", p=128)[:, 4 * qd:4 * (qd + 1), :])
        _wait(p, "vector", s_ld, ld)
        for Ti in range(4):
            T = 4 * qd + Ti
            _op(p, "vector", lambda e, T=T: e.tensor_scalar(
                dump[:], att_tm[:, T, :], stats[:, 2, T:T + 1],
                stats[:, 5, T:T + 1], ALU.subtract, ALU.mult), drain=True)
            _op(p, "vector", lambda e, T=T, Ti=Ti: e.tensor_add(
                x1[:, T, :], xq[:, Ti, :], dump[:]), drain=True)
        prev_q = _op(p, "vector", lambda e: _nop(e), inc=v_ev)
    x1_done = v_ev.n

    for T in range(16):
        _op(p, "vector", lambda e, T=T: e.reduce_sum(
            stats[:, 0, T:T + 1], x1[:, T, :], axis=mybir.AxisListType.X))
        _op(p, "vector", lambda e, T=T: e.tensor_mul(
            dump[:], x1[:, T, :], x1[:, T, :]), drain=True)
        _op(p, "vector", lambda e, T=T: e.reduce_sum(
            stats[:, 1, T:T + 1], dump[:], axis=mybir.AxisListType.X), drain=True)
    _op(p, "vector", lambda e: e.drain())
    _op(p, "vector", lambda e: e.tensor_scalar_mul(stats[:, 2, :], stats[:, 0, :], 1.0 / D), drain=True)
    _op(p, "vector", lambda e: e.tensor_mul(stats[:, 3, :], stats[:, 2, :], stats[:, 2, :]), drain=True)
    vd = _op(p, "vector", lambda e: e.scalar_tensor_tensor(
        stats[:, 4, :], stats[:, 1, :], 1.0 / D, stats[:, 3, :],
        ALU.mult, ALU.subtract), inc=v_ev)
    _wait(p, "scalar", v_ev, vd)
    cd = _op(p, "scalar", lambda e: e.activation(
        out=dump[:, 0:16], in_=stats[:, 4, :], func=AF.Sqrt, bias=eps_sb[:]),
        inc=c_ev, drain=True)
    _wait(p, "vector", c_ev, cd)
    _op(p, "vector", lambda e: e.reciprocal(stats[:, 5, :], dump[:, 0:16]), drain=True)

    tile_cons = {}
    for T in range(16):
        if (T % 2) in tile_cons:
            _wait(p, "vector", s_tr)
            _wait(p, "vector", s_g)
        _op(p, "vector", lambda e, T=T: e.tensor_scalar(
            tile2[:, T % 2, :], x1[:, T, :], stats[:, 2, T:T + 1],
            stats[:, 5, T:T + 1], ALU.subtract, ALU.mult), drain=True)
        tv = _op(p, "vector", lambda e: _nop(e), inc=v_ev)
        _wait(p, "sync", v_ev, tv)
        trv = _dma(p, "sync", s_tr, x1nT[:, :, 128 * T:128 * (T + 1)],
                   tile2[:, T % 2, :], transpose=True)
        _wait(p, "gpsimd", v_ev, tv)
        gv = _dma(p, "gpsimd", s_g,
                  x1n_my.rearrange("(Tp pp) d -> pp Tp d", pp=128)[:, T, :],
                  tile2[:, T % 2, :])
        tile_cons[T % 2] = (trv, gv)


    x1nT_ready = v_ev.n
    x1n_my_done = s_g.n
    _wait(p, "sync", v_ev, x1_done)
    _dma(p, "sync", s_st, x1_hbm, x1[:])
    x1_spilled = s_st.n

    return nc, p, locals()


def emit_bc(nc, p, L):
    g = lambda k: L[k]
    (pb, x1nT, dump, stats, eps_sb, tile2) = (g("pb"), g("x1nT"), g("dump"), g("stats"), g("eps_sb"), g("tile2"))
    s_ld, s_tr, s_st, s_g, g_rs = g("s_ld"), g("s_tr"), g("s_st"), g("s_g"), g("g_rs")
    t_mm, c_ev, v_ev = g("t_mm"), g("c_ev"), g("v_ev")
    ones1, sels = g("ones1"), g("sels")
    vstage, tbuf = g("vstage"), g("tbuf")
    router_w = g("router_w")
    logits, max8, w8, idsf, ids_u = g("logits"), g("max8"), g("w8"), g("idsf"), g("ids_u")
    sm1, sm2 = g("sm1"), g("sm2")
    wloc, wlocT = g("wloc"), g("wlocT")
    a1_in, a1_out, a2_in, a2_out = g("a1_in"), g("a1_out"), g("a2_in"), g("a2_out")
    x1n_my, x1n_all = g("x1n_my"), g("x1n_all")
    wraw, obf, scn, posb, vvv = g("wraw"), g("obf"), g("scn"), g("posb"), g("vvv")
    sidx, sdat, sgat = g("sidx"), g("sdat"), g("sgat")
    tot64, tb8 = g("tot64"), g("tb8")
    tokp1c, e64c, offtc = g("tokp1c"), g("e64c"), g("offtc")
    tokm, gatem, mrg, mrgg = g("tokm"), g("gatem"), g("mrg"), g("mrgg")
    tok8, gate8, idx16, idx8 = g("tok8"), g("gate8"), g("idx16"), g("idx8")
    cnt8, cnt8i = g("cnt8"), g("cnt8i")
    c512i = g("c512i")
    grow, Wb, y0row, ninv_row = g("grow"), g("Wb"), g("y0row"), g("ninv_row")
    tokm_dram, gatem_dram, cnt_dram = g("tokm_dram"), g("gatem_dram"), g("cnt_dram")
    tok8_dram, gate8_dram = g("tok8_dram"), g("gate8_dram")
    ridx_dram, ninv_dram = g("ridx_dram"), g("ninv_dram")
    y_my, y_all = g("y_my"), g("y_all")
    Xe, hbv, yT, ytm = g("Xe"), g("hbv"), g("yT"), g("ytm")
    wslots = g("wslots")
    wg_my, wu_my, wd_my, ws_gu, ws_dn = g("wg_my"), g("wu_my"), g("wd_my"), g("ws_gu"), g("ws_dn")
    routedT, Rk, hsh, ffnT, ffn_tm, x1q = g("routedT"), g("Rk"), g("hsh"), g("ffnT"), g("ffn_tm"), g("x1q")
    Rk4 = g("Rk4")
    posk, maskk, ridx, pft = g("posk"), g("maskk"), g("ridx"), g("pft")
    rowk, offc = g("rowk"), g("offc")
    scr1, scr2, i32a = g("scr1"), g("scr2"), g("i32a")
    x1_hbm, out_my = g("x1_hbm"), g("out_my")
    c_tokp1, c_e64, c_offt = g("c_tokp1"), g("c_e64"), g("c_offt")
    x1nT_ready = g("x1nT_ready")
    x1n_my_done = g("x1n_my_done")

    cc = p.sem("cc")
    g_lib = p.sem("g_lib")
    g_d2 = p.sem("g_d2")
    v_r = p.sem("v_r")
    t_r = p.sem("t_r")
    g_ga = p.sem("g_ga")
    t_e = p.sem("t_e")
    v_e = p.sem("v_e")
    c_e2 = p.sem("c_e2")
    s_y = p.sem("s_y")
    g_rt = p.sem("g_rt")
    v_rt = p.sem("v_rt")

    _dma(p, "sync", s_ld, tokp1c[:], c_tokp1)
    _dma(p, "sync", s_ld, e64c[:], c_e64)
    _dma(p, "sync", s_ld, offtc[:], c_offt)
    _dma(p, "sync", s_ld, g("offc")[:], g("c_off64"))
    c_id = nc.inline_tensor(np.eye(128, dtype=np.float32), "c_id").ap()
    id_sb = nc.alloc_sbuf_tensor_at("id_sb", [128, 128], F32, offset=g("oF") + 31776)
    _dma(p, "sync", s_ld, id_sb[:], c_id)

    _wait(p, "gpsimd", s_g, x1n_my_done)
    _op(p, "gpsimd", lambda e: e.collective_compute(
        "AllGather", ALU.bypass, replica_groups=[list(range(8))],
        ins=[x1n_my], outs=[x1n_all]), inc=cc)
    ag_x_done = cc.n

    def _bc_fin():
        _wait(p, "sync", s_st, s_st.n)
        _wait(p, "sync", s_g, s_g.n)
        _wait(p, "sync", s_tr, s_tr.n)
        _wait(p, "sync", s_ld, s_ld.n)
        p.finalize()
    MKBC = os.environ.get("MKBC", "full")
    if MKBC == "b0":
        _bc_fin(); return dict(locals())
    _dma(p, "gpsimd", g_d2, vstage[0:64, 0, :], router_w)
    _wait(p, "sync", g_d2)
    _dma(p, "sync", s_tr, tbuf[:].rearrange("p c q -> p c q")[:, :, 0:64],
         vstage[0:64, 0, :], transpose=True)
    rwT_done = s_tr.n
    rwT = tbuf

    _wait(p, "tensor", s_tr, rwT_done)
    _wait(p, "tensor", v_ev, x1nT_ready)
    _wait(p, "tensor", c_ev, c_ev.n)
    _wait(p, "tensor", s_ld, s_ld.n)
    lgT = nc.alloc_sbuf_tensor_at("lgT", [64, 4, 512], F32, offset=g("oB0") + 2048)
    for tq in range(4):
        for dc in range(8):
            tv = _op(p, "tensor", lambda e, tq=tq, dc=dc: e.matmul(
                pb[tq][0:64, :], rwT[:, dc, 0:64],
                x1nT[:, dc, 512 * tq:512 * (tq + 1)],
                start=(dc == 0), stop=(dc == 7)), inc=(t_r if dc == 7 else None))
        _wait(p, "vector", t_r, tv)
        _op(p, "vector", lambda e, tq=tq: e.tensor_copy(lgT[:, tq, :], pb[tq][0:64, :]),
            inc=v_r)
    lgT_done = v_r.n
    _wait(p, "tensor", v_r, lgT_done)
    lg_ev = {}
    for T in range(16):
        b = 4 + (T % 4)
        if T >= 4:
            _wait(p, "tensor", v_r, lg_ev[b - 4])
        tv = _op(p, "tensor", lambda e, T=T, b=b: e.transpose(
            pb[b][:, 0:64], lgT[:, T // 4, 128 * (T % 4):128 * (T % 4) + 128],
            id_sb[0:64, 0:64]), inc=t_r)
        _wait(p, "vector", t_r, tv)
        ev = _op(p, "vector", lambda e, T=T, b=b: e.tensor_copy(
            logits[:, T, :], pb[b][:, 0:64]), inc=v_r)
        lg_ev[b - 4] = ev
    logits_done = v_r.n

    if MKBC == "b1":
        _bc_fin(); return dict(locals())
    for T in range(16):
        _op(p, "vector", lambda e, T=T: e.max(out=max8[:, T, :], in_=logits[:, T, :]), drain=True)
        _op(p, "vector", lambda e, T=T: e.max_index(
            out=ids_u[:, T, :], in_max=max8[:, T, :], in_values=logits[:, T, :]))
    _op(p, "vector", lambda e: e.drain())
    _op(p, "vector", lambda e: e.tensor_copy(idsf[:], ids_u[:]), drain=True)
    _op(p, "vector", lambda e: e.reduce_max(sm1[:].rearrange("p T -> p T ()"), max8[:], axis=mybir.AxisListType.X)
        if False else e.reduce_max(sm1[:], max8[:].rearrange("p a b -> p a b"), axis=mybir.AxisListType.X), drain=True)
    _op(p, "vector", lambda e: e.tensor_scalar_mul(sm2[:], sm1[:], -1.0), drain=True)
    vv = _op(p, "vector", lambda e: _nop(e), inc=v_r)
    _wait(p, "scalar", v_r, vv)
    for T in range(16):
        cv = _op(p, "scalar", lambda e, T=T: e.activation(
            out=w8[:, T, :], in_=max8[:, T, :], func=AF.Exp,
            bias=sm2[:, T:T + 1]), inc=c_ev, drain=True)
    _wait(p, "vector", c_ev, c_ev.n)
    _op(p, "vector", lambda e: e.reduce_sum(sm1[:], w8[:].rearrange("p a b -> p a b"), axis=mybir.AxisListType.X), drain=True)
    _op(p, "vector", lambda e: e.reciprocal(sm2[:], sm1[:]), drain=True)
    for T in range(16):
        _op(p, "vector", lambda e, T=T: e.tensor_scalar(
            w8[:, T, :], w8[:, T, :], sm2[:, T:T + 1], None, ALU.mult), drain=True)
    _op(p, "vector", lambda e: e.tensor_add(scr1[:].rearrange("p a b -> p (a b)"),
                                            idsf[:].rearrange("p a b -> p (a b)"), offtc[:]), drain=True)
    _op(p, "vector", lambda e: e.tensor_single_scalar(
        scr2[:].rearrange("p a b -> p (a b)"), scr1[:].rearrange("p a b -> p (a b)"),
        1024.0, ALU.is_lt), drain=True)
    _op(p, "vector", lambda e: e.scalar_tensor_tensor(
        dump[:, 0:128], scr1[:].rearrange("p a b -> p (a b)"), 1.0,
        scr2[:].rearrange("p a b -> p (a b)"), ALU.add, ALU.mult), drain=True)
    _op(p, "vector", lambda e: e.tensor_scalar_add(dump[:, 0:128], dump[:, 0:128], -1.0), drain=True)
    _op(p, "vector", lambda e: e.tensor_copy(sidx[:, 0:128], dump[:, 0:128]), drain=True)
    _op(p, "vector", lambda e: e.tensor_scalar(
        dump[:, 0:128], scr2[:].rearrange("p a b -> p (a b)"), 1.0, -1.0,
        ALU.subtract, ALU.mult), drain=True)
    _op(p, "vector", lambda e: e.scalar_tensor_tensor(
        dump[:, 128:256], scr1[:].rearrange("p a b -> p (a b)"), 1.0 - 1024.0,
        dump[:, 0:128], ALU.add, ALU.mult), drain=True)
    _op(p, "vector", lambda e: e.tensor_scalar_add(dump[:, 128:256], dump[:, 128:256], -1.0), drain=True)
    _op(p, "vector", lambda e: e.tensor_copy(sidx[:, 128:256], dump[:, 128:256]), drain=True)
    _op(p, "vector", lambda e: e.tensor_copy(sgat[:, 0:128], w8[:].rearrange("p a b -> p (a b)")), drain=True)
    w_sc_in = _op(p, "vector", lambda e: _nop(e), inc=v_r)

    if MKBC == "b2":
        _bc_fin(); return dict(locals())
    _op(p, "gpsimd", lambda e: e.load_library(LC.local_scatter))
    _wait(p, "gpsimd", v_r, w_sc_in)
    sc1 = _op(p, "gpsimd", lambda e: e.local_scatter(
        wloc[:, 0:8, :].rearrange("p a b -> p (a b)"), sgat[:, 0:128], sidx[:, 0:128],
        128, 1024, 128), inc=g_lib)
    sc1 = _op(p, "gpsimd", lambda e: e.local_scatter(
        wloc[:, 8:16, :].rearrange("p a b -> p (a b)"), sgat[:, 0:128], sidx[:, 128:256],
        128, 1024, 128), inc=g_lib)
    _wait(p, "sync", g_lib, sc1)
    for T in range(16):
        _dma(p, "sync", s_tr, wlocT[:, T, :].rearrange("p q -> p () q"),
             wloc[:, T, :], transpose=True)
    _wait(p, "sync", s_tr, s_tr.n)
    _dma(p, "sync", s_st, a1_in, wlocT[0:64, :, :])
    a1_in_done = s_st.n
    _wait(p, "gpsimd", s_st, a1_in_done)
    _op(p, "gpsimd", lambda e: e.collective_compute(
        "AllToAll", ALU.bypass, replica_groups=[list(range(8))],
        ins=[a1_in], outs=[a1_out]), inc=cc)
    a1_done = cc.n

    if MKBC == "b3":
        _bc_fin(); return dict(locals())
    _wait(p, "sync", cc, a1_done)
    ld1 = _dma(p, "sync", s_ld, wraw[:], a1_out.rearrange("r (h c) -> (r h) c", h=2))
    _wait(p, "vector", s_ld, ld1)
    _op(p, "vector", lambda e: e.tensor_single_scalar(obf[:], wraw[:], 0.0, ALU.is_gt), drain=True)
    _op(p, "vector", lambda e: e.memset(vvv[:], 0.0), drain=True)
    _op(p, "vector", lambda e: e.tensor_tensor_scan(scn[:], obf[:], vvv[:], 0.0,
                                                    ALU.add, ALU.add), drain=True)
    _op(p, "vector", lambda e: e.tensor_copy(tot64[:], scn[:, 1023:1024]), drain=True)
    tot_done = _op(p, "vector", lambda e: _nop(e), inc=v_r)
    _wait(p, "sync", v_r, tot_done)
    _dma(p, "sync", s_st, cnt_dram, tot64[:])
    _wait(p, "sync", s_st, s_st.n)
    _dma(p, "sync", s_ld, tb8[:].rearrange("j (b h) -> j b h", h=2),
         cnt_dram.rearrange("r o -> (r o)").rearrange("(b j h) -> j b h", j=8, h=2))
    ld2 = s_ld.n
    _wait(p, "vector", s_ld, ld2)
    tb8s, tb8z = g("tb8s"), g("tb8z")
    _op(p, "vector", lambda e: e.memset(tb8z[:], 0.0), drain=True)
    _op(p, "vector", lambda e: e.tensor_tensor_scan(tb8s[:], tb8[:], tb8z[:], 0.0,
                                                    ALU.add, ALU.add), drain=True)
    _op(p, "vector", lambda e: e.tensor_sub(tb8s[:], tb8s[:], tb8[:]), drain=True)
    base_done = _op(p, "vector", lambda e: _nop(e), inc=v_r)
    _wait(p, "sync", v_r, base_done)
    _dma(p, "sync", s_st, cnt_dram.rearrange("(j bh) o -> j (bh o)", j=8), tb8s[:])
    _wait(p, "sync", s_st, s_st.n)
    for b_ in range(8):
        _dma(p, "sync", s_ld, tot64[16 * b_:16 * (b_ + 1), :],
             cnt_dram.rearrange("r o -> (r o)").rearrange("(j b h) -> b j h", j=8, h=2)[b_])
    ld3 = s_ld.n
    _wait(p, "vector", s_ld, ld3)
    _op(p, "vector", lambda e: e.tensor_sub(posb[:], scn[:], obf[:]), drain=True)
    _op(p, "vector", lambda e: e.tensor_scalar(posb[:], posb[:], tot64[:], None, ALU.add), drain=True)
    _op(p, "vector", lambda e: e.tensor_single_scalar(vvv[:], posb[:], 512.0, ALU.is_lt), drain=True)
    _op(p, "vector", lambda e: e.tensor_mul(vvv[:], vvv[:], obf[:]), drain=True)
    _op(p, "vector", lambda e: e.tensor_scalar_add(scn[:], posb[:], 1.0), drain=True)
    _op(p, "vector", lambda e: e.tensor_mul(scn[:], scn[:], vvv[:]), drain=True)
    _op(p, "vector", lambda e: e.tensor_scalar_add(scn[:], scn[:], -1.0), drain=True)
    _op(p, "vector", lambda e: e.tensor_copy(sidx[:], scn[:]), drain=True)
    _op(p, "vector", lambda e: e.tensor_copy(sdat[:], tokp1c[:]), drain=True)
    _op(p, "vector", lambda e: e.tensor_copy(sgat[:], wraw[:]), drain=True)
    _op(p, "vector", lambda e: e.tensor_copy(wraw[:], scn[:]), drain=True)
    _op(p, "vector", lambda e: e.reduce_sum(tot64[:], vvv[:], axis=mybir.AxisListType.X), drain=True)
    prep_done = _op(p, "vector", lambda e: _nop(e), inc=v_r)
    _wait(p, "sync", v_r, prep_done)
    _dma(p, "sync", s_st, a2_in.rearrange("r (h c) -> (r h) c", h=2), wraw[:])
    _dma(p, "sync", s_st, cnt_dram, tot64[:])
    a2_in_done = s_st.n
    _wait(p, "gpsimd", s_st, a2_in_done)
    _op(p, "gpsimd", lambda e: e.collective_compute(
        "AllToAll", ALU.bypass, replica_groups=[list(range(8))],
        ins=[a2_in], outs=[a2_out]), inc=cc)
    a2_done = cc.n
    if MKBC == "b4":
        _bc_fin(); return dict(locals())
    _wait(p, "gpsimd", v_r, prep_done)
    sc2 = _op(p, "gpsimd", lambda e: e.local_scatter(
        tokm[:], sdat[:], sidx[:], 128, 512, 1024), inc=g_lib)
    sc3 = _op(p, "gpsimd", lambda e: e.local_scatter(
        gatem[:], sgat[:], sidx[:], 128, 512, 1024), inc=g_lib)
    _wait(p, "gpsimd", g_lib, sc3)
    _dma(p, "gpsimd", g_d2, tokm_dram, tokm[:])
    _dma(p, "gpsimd", g_d2, gatem_dram, gatem[:])
    scat_stored = g_d2.n

    if MKBC == "b5":
        _bc_fin(); return dict(locals())
    _wait(p, "sync", g_d2, scat_stored)
    for j in range(8):
        _dma(p, "sync", s_ld, mrg[j:j + 1, :, :],
             tokm_dram.rearrange("(b rr) c -> b rr c", rr=16)[:, 2 * j:2 * j + 2, :])
    mrg_ld = s_ld.n
    _wait(p, "vector", s_ld, mrg_ld)
    def _red_tok(e):
        with nc.allow_low_precision("exact one-hot int16 merge"):
            return e.tensor_reduce(tok8[:], mrg[:].rearrange("j bh c -> j c bh"),
                                   mybir.AxisListType.X, ALU.add)
    _op(p, "vector", _red_tok, drain=True)
    def _sub_tok(e):
        with nc.allow_low_precision("int16 sub"):
            return e.tensor_single_scalar(tok8[:], tok8[:], 1, ALU.subtract)
    _op(p, "vector", _sub_tok, drain=True)
    def _clamp_tok(e):
        with nc.allow_low_precision("int16 max"):
            return e.tensor_single_scalar(tok8[:], tok8[:], 0, ALU.max)
    _op(p, "vector", _clamp_tok, drain=True)
    tok8_done = _op(p, "vector", lambda e: _nop(e), inc=v_r)
    _wait(p, "sync", v_r, tok8_done)
    for j in range(8):
        _dma(p, "sync", s_ld, mrgg[j:j + 1, :, :],
             gatem_dram.rearrange("(b rr) c -> b rr c", rr=16)[:, 2 * j:2 * j + 2, :])
    mrgg_ld = s_ld.n
    _wait(p, "vector", s_ld, mrgg_ld)
    def _red_gate(e):
        with nc.allow_low_precision("exact one-hot f16 merge"):
            return e.tensor_reduce(gate8[:], mrgg[:].rearrange("j bh c -> j c bh"),
                                   mybir.AxisListType.X, ALU.add)
    _op(p, "vector", _red_gate, drain=True)
    gate8_done = _op(p, "vector", lambda e: _nop(e), inc=v_r)

    _wait(p, "sync", v_r, gate8_done)
    _dma(p, "sync", s_st, tok8_dram, tok8[:])
    _dma(p, "sync", s_st, gate8_dram, gate8[:])
    t8_stored = s_st.n
    zi = _op(p, "vector", lambda e: e.memset(idx16[:], 0), inc=v_r, drain=True)
    _wait(p, "sync", v_r, zi)
    _wait(p, "sync", s_st, t8_stored)
    _dma(p, "sync", s_ld, idx16[0:16, :],
         tok8_dram.rearrange("e (pd pl) -> pl (e pd)", pl=16),
         allow_slow_non_contiguous=True)
    _wait(p, "sync", s_ld, s_ld.n)
    _dma(p, "sync", s_ld, idx16[16:32, :], idx16[0:16, :])
    _dma(p, "sync", s_ld, grow[:], gate8_dram.rearrange("e c -> () (e c)"))
    _dma(p, "sync", s_ld, dump[0:1, 0:128],
         cnt_dram.rearrange("(r) o -> () (r o)"))
    cnt_ld = s_ld.n
    _wait(p, "vector", s_ld, cnt_ld)
    _op(p, "vector", lambda e: e.tensor_reduce(
        cnt8[:], dump[0:1, 0:128].rearrange("o (b j h) -> o j b h", j=8, h=2),
        mybir.AxisListType.XY, ALU.add), drain=True)
    _op(p, "vector", lambda e: e.tensor_copy(cnt8i[:], cnt8[:]), drain=True)
    _op(p, "vector", lambda e: e.memset(c512i[:], 512), drain=True)
    cnt_done = _op(p, "vector", lambda e: _nop(e), inc=v_r)
    return dict(locals())


def emit_c(nc, p, L, B):
    g = lambda k: L[k] if k in L else B[k]
    pb, dump, stats, eps_sb = g("pb"), g("dump"), g("stats"), g("eps_sb")
    s_ld, s_tr, s_st, s_g = g("s_ld"), g("s_tr"), g("s_st"), g("s_g")
    t_mm, c_ev, v_ev = g("t_mm"), g("c_ev"), g("v_ev")
    ones1 = g("ones1")
    cc, g_lib, g_d2, v_r, t_r = B["cc"], B["g_lib"], B["g_d2"], B["v_r"], B["t_r"]
    g_ga, t_e, v_e, c_e2, s_y = B["g_ga"], B["t_e"], B["v_e"], B["c_e2"], B["s_y"]
    g_rt, v_rt = B["g_rt"], B["v_rt"]
    x1nT = g("x1nT")
    Xe, hbv, yT, ytm = g("Xe"), g("hbv"), g("yT"), g("ytm")
    wslots = g("wslots")
    wg_my, wu_my, wd_my, ws_gu, ws_dn = g("wg_my"), g("wu_my"), g("wd_my"), g("ws_gu"), g("ws_dn")
    x1n_all, y_my, y_all = g("x1n_all"), g("y_my"), g("y_all")
    idx16, idx8, cnt8i, grow, Wb = g("idx16"), g("idx8"), g("cnt8i"), g("grow"), g("Wb")
    c512i = g("c512i")
    y0row, ninv_row = g("y0row"), g("ninv_row")
    routedT, Rk, hsh, ffnT, ffn_tm, x1q = g("routedT"), g("Rk"), g("hsh"), g("ffnT"), g("ffn_tm"), g("x1q")
    Rk4 = g("Rk4")
    posk, maskk, ridx, pft = g("posk"), g("maskk"), g("ridx"), g("pft")
    rowk, offc = g("rowk"), g("offc")
    scr1, scr2, i32a = g("scr1"), g("scr2"), g("i32a")
    idsf, sm1, sm2, e64c = g("idsf"), g("sm1"), g("sm2"), g("e64c")
    a2_out = g("a2_out")
    ridx_dram, ninv_dram = g("ridx_dram"), g("ninv_dram")
    x1_hbm, out_my = g("x1_hbm"), g("out_my")
    tbuf = g("tbuf")
    tile2 = g("tile2")
    oC = g("oC")
    ag_x_done, a2_done = B["ag_x_done"], B["a2_done"]
    cnt_done = B["cnt_done"]

    _op(p, "gpsimd", lambda e: e.load_library(LC.mlp), inc=g_lib)
    slot_free = {}
    wld = {}
    SERV = os.environ.get("MKSER", "0")
    SER = SERV == "1"
    SER_W = SER or SERV == "2"
    SER_G = SER or SERV == "3"
    SER_T = SER or SERV == "4"

    def w_load(e_, which):
        sl = (3 * e_ + which) % 5
        if sl in slot_free:
            sem, val = slot_free[sl]
            _wait(p, "gpsimd", sem, val)
        if which == 0:
            _dma(p, "gpsimd", g_d2, wslots[sl][:],
                 wg_my[e_].rearrange("(dc pp) f -> pp dc f", pp=128))
        elif which == 1:
            _dma(p, "gpsimd", g_d2, wslots[sl][:],
                 wu_my[e_].rearrange("(dc pp) f -> pp dc f", pp=128))
        else:
            _dma(p, "gpsimd", g_d2, wslots[sl][:].rearrange("p a b -> p (a b)")[:, 0:6144]
                 .rearrange("p (fc dd) -> p fc dd", fc=6),
                 wd_my[e_].rearrange("(fc pp) dd -> pp fc dd", pp=128))
        wld[(e_, which)] = g_d2.n
        return sl

    zxe = _op(p, "vector", lambda e: e.memset(
        Xe[:].rearrange("p a b c -> p (a b c)"), 0), inc=v_r, drain=True)
    _wait(p, "gpsimd", cc, ag_x_done)
    _wait(p, "gpsimd", v_r, cnt_done)
    _wait(p, "gpsimd", v_r, zxe)
    xe_free = {}
    ga_done = []
    rv512 = [None]
    def _ld512(eng):
        rv512[0] = eng.value_load(in_tensor=c512i[0:1, 0:1], min_val=0, max_val=512)
    p.emit("gpsimd", _ld512)

    def emit_gather(e_):
        par = e_ % 2
        if SER_G:
            _wait(p, "gpsimd", v_e, v_e.n)
            _wait(p, "gpsimd", s_y, s_y.n)
            _wait(p, "gpsimd", t_e, t_e.n)
        _wait(p, "gpsimd", g_d2, g_d2.n)
        if par in xe_free:
            _wait(p, "gpsimd", t_e, xe_free[par])
        def do_gather(eng, e_=e_, par=par):
            eng.dma_gather(Xe[:, par, :, :], x1n_all, idx16[:, 32 * e_:32 * (e_ + 1)],
                           512, rv512[0], 1024, transpose=True).then_inc(g_ga.h, 16)
        p.emit("gpsimd", do_gather)
        g_ga.inc(16)
        _wait(p, "gpsimd", g_ga, g_ga.n)
        ga_done.append(g_ga.n)

    emit_gather(0)
    for wh in range(3):
        w_load(0, wh)
    emit_gather(1)
    w_load(1, 0)
    w_load(1, 1)

    _wait(p, "tensor", s_ld, s_ld.n)
    _wait(p, "tensor", v_r, cnt_done)
    ywr = []
    for e_ in range(8):
        par = e_ % 2
        slg, slu, sld_ = (3 * e_) % 5, (3 * e_ + 1) % 5, (3 * e_ + 2) % 5
        if SER_T:
            _wait(p, "tensor", s_y, s_y.n)
            _wait(p, "tensor", c_e2, c_e2.n)
            _wait(p, "tensor", g_d2, g_d2.n)
            _wait(p, "tensor", g_ga, g_ga.n)
        _wait(p, "tensor", g_d2, wld[(e_, 0)])
        _wait(p, "tensor", g_ga, ga_done[e_])
        _wait(p, "tensor", v_e, v_e.n)
        tv = _op(p, "tensor", lambda e, e_=e_: e.matmul(
            pb[7][:, :], ones1[:, :], grow[:, 512 * e_:512 * (e_ + 1)],
            start=True, stop=True), inc=t_e)
        _wait(p, "vector", t_e, tv)
        wb_ev = _op(p, "vector", lambda e: e.tensor_copy(Wb[:], pb[7][:, :]), inc=v_e)
        for which, wslot in ((0, slg), (1, slu)):
            if which == 1:
                _wait(p, "tensor", g_d2, wld[(e_, 1)])
            for fc in range(6):
                for dc in range(8):
                    tv = _op(p, "tensor", lambda e, wslot=wslot, fc=fc, dc=dc, par=par: e.matmul(
                        pb[fc][:, :], wslots[wslot][:, dc, 128 * fc:128 * (fc + 1)],
                        Xe[:, par, dc, :], start=(dc == 0), stop=(dc == 7)),
                        inc=(t_e if dc == 7 else None))
                if which == 1 and fc == 0:
                    pass
                if which == 0:
                    _wait(p, "scalar", t_e, tv)
                    _op(p, "scalar", lambda e, fc=fc: e.activation(
                        out=hbv[:, fc, :], in_=pb[fc][:, :], func=AF.Silu), inc=c_e2)
                else:
                    _wait(p, "vector", t_e, tv)
                    _wait(p, "vector", c_e2, c_e2.n)
                    if fc == 0:
                        _wait(p, "vector", v_e, wb_ev)
                    _op(p, "vector", lambda e, fc=fc: e.tensor_mul(
                        dump[:, 0:512], hbv[:, fc, :], pb[fc][:, :]), drain=True)
                    _op(p, "vector", lambda e, fc=fc: e.tensor_mul(
                        hbv[:, fc, :], dump[:, 0:512], Wb[:]), drain=True)
                    _op(p, "vector", lambda e: _nop(e), inc=v_e)
            if which == 0:
                _wait(p, "tensor", c_e2, c_e2.n)
        h_done = v_e.n
        _wait(p, "tensor", g_d2, wld[(e_, 2)])
        _wait(p, "tensor", v_e, h_done)
        dn_ev = {}
        for ddc in range(8):
            bk = ddc % 4
            if bk in dn_ev:
                _wait(p, "tensor", v_e, dn_ev[bk])
            for fc in range(6):
                tv = _op(p, "tensor", lambda e, sld_=sld_, ddc=ddc, fc=fc, bk=bk: e.matmul(
                    pb[bk][:, :], wslots[sld_][:].rearrange("p a b -> p (a b)")[:, 0:6144]
                    .rearrange("p (fc dd) -> p fc dd", fc=6)[:, fc, 128 * ddc:128 * (ddc + 1)],
                    hbv[:, fc, :], start=(fc == 0), stop=(fc == 5)),
                    inc=(t_e if fc == 5 else None))
            _wait(p, "vector", t_e, tv)
            dn_ev[bk] = _op(p, "vector", lambda e, ddc=ddc, bk=bk: e.tensor_copy(
                yT[:, ddc, :], pb[bk][:, :]), inc=v_e)
        ye_done = v_e.n
        slot_free[slg] = (t_e, t_e.n)
        slot_free[slu] = (t_e, t_e.n)
        slot_free[sld_] = (t_e, t_e.n)
        xe_free[par] = t_e.n
        if e_ == 4 and os.environ.get("MKAG1", "0") == "1":
            _wait(p, "gpsimd", s_y, ywr[3])
            _op(p, "gpsimd", lambda e: e.collective_compute(
                "AllGather", ALU.bypass, replica_groups=[list(range(8))],
                ins=[y_my[0:2048]], outs=[y_all[0:16384]]), inc=cc)
        if e_ + 2 <= 7:
            emit_gather(e_ + 2)
        if e_ + 1 <= 7:
            w_load(e_ + 1, 2)
        if e_ + 2 <= 7:
            w_load(e_ + 2, 0)
            w_load(e_ + 2, 1)
        _wait(p, "sync", v_e, ye_done)
        for ddc in range(8):
            _dma(p, "sync", s_y, ytm[:, :, 128 * ddc:128 * (ddc + 1)], yT[:, ddc, :],
                 transpose=True)
        _wait(p, "sync", s_y, s_y.n)
        _dma(p, "sync", s_y, y_my[512 * e_:512 * (e_ + 1), :].rearrange("(sb pp) d -> pp sb d", pp=128),
             ytm[:])
        ywr.append(s_y.n)
        _wait(p, "vector", s_y, s_y.n)

    wsgu = nc.alloc_sbuf_tensor_at("wsgu", [128, 8, 1536], BF16, offset=oC)
    wsdn = nc.alloc_sbuf_tensor_at("wsdn", [128, 6, 1024], BF16, offset=g("oB0") + 14336)
    _wait(p, "gpsimd", t_e, t_e.n)
    _dma(p, "gpsimd", g_d2, wsgu[:], ws_gu.rearrange("(dc pp) f -> pp dc f", pp=128))
    _dma(p, "gpsimd", g_d2, wsdn[:], ws_dn.rearrange("(fc pp) dd -> pp fc dd", pp=128))
    wsh_ld = g_d2.n

    _wait(p, "gpsimd", s_y, ywr[-1])
    if os.environ.get("MKAG1", "0") != "1":
        _op(p, "gpsimd", lambda e: e.collective_compute(
            "AllGather", ALU.bypass, replica_groups=[list(range(8))],
            ins=[y_my[0:2048]], outs=[y_all[0:16384]]), inc=cc)
    _op(p, "gpsimd", lambda e: e.collective_compute(
        "AllGather", ALU.bypass, replica_groups=[list(range(8))],
        ins=[y_my[2048:4096]], outs=[y_all[16384:32768]]), inc=cc)
    ag_y_done = cc.n

    _wait(p, "sync", cc, a2_done)
    tr_pf = _dma(p, "sync", s_tr, pft[:, :, 0:64], a2_out, transpose=True)
    _wait(p, "vector", s_tr, tr_pf)
    for T in range(16):
        _op(p, "vector", lambda e, T=T: e.tensor_tensor(
            dump[:, 0:512].rearrange("p (k e) -> p k e", k=8),
            idsf[:, T, :].rearrange("p k -> p k ()").broadcast_to([128, 8, 64]),
            e64c[:].rearrange("p e -> p () e").broadcast_to([128, 8, 64]),
            ALU.is_equal), drain=True)
        _op(p, "vector", lambda e, T=T: e.tensor_tensor(
            dump[:, 512:1024].rearrange("p (k e) -> p k e", k=8),
            dump[:, 0:512].rearrange("p (k e) -> p k e", k=8),
            offc[:, 0:64].rearrange("p e -> p () e").broadcast_to([128, 8, 64]),
            ALU.mult), drain=True)
        _op(p, "vector", lambda e, T=T: e.tensor_reduce(
            rowk[:, T, :], dump[:, 512:1024].rearrange("p (k e) -> p k e", k=8),
            mybir.AxisListType.X, ALU.add), drain=True)
        _op(p, "vector", lambda e, T=T: e.tensor_tensor(
            dump[:, 0:512].rearrange("p (k e) -> p k e", k=8),
            dump[:, 0:512].rearrange("p (k e) -> p k e", k=8),
            pft[:, T, 0:64].rearrange("p e -> p () e").broadcast_to([128, 8, 64]),
            ALU.mult), drain=True)
        _op(p, "vector", lambda e, T=T: e.tensor_reduce(
            posk[:, T, :], dump[:, 0:512].rearrange("p (k e) -> p k e", k=8),
            mybir.AxisListType.X, ALU.add), drain=True)
    _op(p, "vector", lambda e: e.tensor_single_scalar(
        maskk[:].rearrange("p a b -> p (a b)"), posk[:].rearrange("p a b -> p (a b)"),
        0.0, ALU.is_ge), drain=True)
    _op(p, "vector", lambda e: e.tensor_add(
        scr1[:].rearrange("p a b -> p (a b)"), rowk[:].rearrange("p a b -> p (a b)"),
        posk[:].rearrange("p a b -> p (a b)")), drain=True)
    _op(p, "vector", lambda e: e.tensor_mul(
        scr2[:].rearrange("p a b -> p (a b)"), scr1[:].rearrange("p a b -> p (a b)"),
        maskk[:].rearrange("p a b -> p (a b)")), drain=True)

    _op(p, "vector", lambda e: e.tensor_copy(
        ridx[:].rearrange("p k T -> p T k"), scr2[:]), drain=True)
    _op(p, "vector", lambda e: e.tensor_reduce(
        sm1[:, 0:8], maskk[:].rearrange("p T k -> p k T"), mybir.AxisListType.X,
        ALU.add), drain=True)
    _op(p, "vector", lambda e: e.tensor_reduce(
        sm2[:], maskk[:], mybir.AxisListType.X, ALU.add), drain=True)
    _op(p, "vector", lambda e: e.tensor_scalar_add(sm2[:], sm2[:], -8.0), drain=True)
    _op(p, "vector", lambda e: e.memset(tile2[:, 0, :], 0.0), drain=True)
    _op(p, "vector", lambda e: e.tensor_copy(tile2[:, 0, 0:16], sm2[:]), drain=True)
    rp_done = _op(p, "vector", lambda e: _nop(e), inc=v_rt)
    _wait(p, "sync", v_rt, rp_done)
    _dma(p, "sync", s_tr, tbuf[:].rearrange("p a b -> p (a b)")[:, 0:128]
         .rearrange("p (o q) -> p o q", o=1), tile2[:, 0, 0:128], transpose=True)
    _wait(p, "sync", s_tr, s_tr.n)
    _dma(p, "sync", s_st, ninv_dram, tbuf[:].rearrange("p a b -> p (a b)")[0:16, 0:128])
    _wait(p, "sync", s_st, s_st.n)
    _dma(p, "sync", s_ld, ninv_row[:], ninv_dram.rearrange("a b -> () (a b)"))
    zi8 = _op(p, "vector", lambda e: e.memset(idx8[:], 0), inc=v_rt, drain=True)
    _wait(p, "sync", v_rt, zi8)
    _dma(p, "sync", s_st, ridx_dram, ridx[:])
    _wait(p, "sync", s_st, s_st.n)
    for k_ in range(8):
        for ph_ in range(8):
            _dma(p, "sync", s_ld,
                 idx8[0:16, k_, :].rearrange("pl (T ph) -> pl T ph", ph=8)[:, :, ph_:ph_ + 1],
                 ridx_dram[16 * ph_:16 * (ph_ + 1), k_, :],
                 allow_slow_non_contiguous=True)
    _wait(p, "sync", s_ld, s_ld.n)
    _dma(p, "sync", s_ld, idx8[16:32, :, :], idx8[0:16, :, :])
    _dma(p, "sync", s_st, ninv_dram.rearrange("a b -> (a b) ()")[0:128, :]
         if False else g("cnt_dram"), sm1[:, 0:1])
    nv_dram = nc.dram_tensor("nv_dram", [128, 8], F32, kind="Internal").ap()
    _wait(p, "sync", v_rt, rp_done)
    _dma(p, "sync", s_st, nv_dram, sm1[:, 0:8])
    _wait(p, "sync", s_st, s_st.n)
    _dma(p, "sync", s_ld, dump[0:1, 0:1024].rearrange("o (p k) -> o p k", p=128), nv_dram)
    nv_ld = s_ld.n
    _wait(p, "vector", s_ld, nv_ld)
    _op(p, "vector", lambda e: e.tensor_reduce(
        g("cnt8")[:],
        dump[0:1, 0:1024].rearrange("o (p k) -> o k p", p=128),
        mybir.AxisListType.X, ALU.add), drain=True)
    _op(p, "vector", lambda e: e.tensor_copy(cnt8i[:], B["cnt8"][:]), drain=True)
    nv_done = _op(p, "vector", lambda e: _nop(e), inc=v_rt)

    _wait(p, "tensor", g_d2, wsh_ld)
    _wait(p, "tensor", v_e, v_e.n)
    _wait(p, "scalar", v_rt, v_rt.n)
    _wait(p, "scalar", s_st, s_st.n)
    _wait(p, "scalar", s_ld, s_ld.n)
    _wait(p, "scalar", s_y, s_y.n)
    _wait(p, "vector", v_rt, v_rt.n)
    _wait(p, "vector", s_st, s_st.n)
    bank_last = {}
    for fc in range(12):
        is_gate = fc < 6
        for tq in range(4):
            bk = (fc * 4 + tq) % 6
            if bk in bank_last:
                bsem, bval = bank_last[bk]
                _wait(p, "tensor", bsem, bval)
            for dc in range(8):
                tv = _op(p, "tensor", lambda e, fc=fc, tq=tq, dc=dc, bk=bk: e.matmul(
                    pb[bk][:, :], wsgu[:, dc, 128 * fc:128 * (fc + 1)],
                    x1nT[:, dc, 512 * tq:512 * (tq + 1)],
                    start=(dc == 0), stop=(dc == 7)), inc=(t_e if dc == 7 else None))
            if is_gate:
                _wait(p, "scalar", t_e, tv)
                _op(p, "scalar", lambda e, fc=fc, tq=tq, bk=bk: e.activation(
                    out=hsh[:, fc, 512 * tq:512 * (tq + 1)], in_=pb[bk][:, :],
                    func=AF.Silu), inc=c_e2)
                bank_last[bk] = ("c", c_e2.n)
            else:
                _wait(p, "vector", t_e, tv)
                _wait(p, "vector", c_e2, c_e2.n)
                vv = _op(p, "vector", lambda e, fc=fc, tq=tq, bk=bk: e.tensor_mul(
                    hsh[:, fc - 6, 512 * tq:512 * (tq + 1)],
                    hsh[:, fc - 6, 512 * tq:512 * (tq + 1)], pb[bk][:, :]), inc=v_e,
                    drain=True)
                bank_last[bk] = ("v", vv)
            sem, val = (c_e2, bank_last[bk][1]) if bank_last[bk][0] == "c" else (v_e, bank_last[bk][1])
            bank_last[bk] = (sem, val)
    hsh_done_v = v_e.n
    hsh_done_c = c_e2.n

    _wait(p, "sync", s_ld, s_ld.n)
    _wait(p, "sync", cc, ag_y_done)
    _dma(p, "sync", s_ld, y0row[:], y_all[0:1, :])

    _wait(p, "gpsimd", cc, ag_y_done)
    _wait(p, "gpsimd", v_rt, nv_done)
    _wait(p, "gpsimd", s_ld, s_ld.n)
    g_rt2 = p.sem("g_rt2")
    rsems = [g_rt, g_rt2]
    half_done = {}
    for k in range(8):
        for h in range(2):
            sem = rsems[h]
            if k >= 1:
                _wait(p, "gpsimd", v_rt, half_done[(k - 1, h)])
            for ck in (2 * h, 2 * h + 1):
                def do_rt1(eng, k=k, ck=ck, sem=sem):
                    eng.dma_gather(Rk4[:, ck, :, :], y_all, idx8[:, k, 32 * ck:32 * (ck + 1)],
                                   512, rv512[0], 1024, transpose=True).then_inc(sem.h, 16)
                p.emit("gpsimd", do_rt1)
                sem.inc(16)
        for h in range(2):
            _wait(p, "vector", rsems[h], 32 * (k + 1))
            rkv = Rk4[:, 2 * h:2 * h + 2, :, :].rearrange("p q d c -> p d q c")
            rtv = routedT[:].rearrange("p d (q c) -> p d q c", q=4)[:, :, 2 * h:2 * h + 2, :]
            if k == 0:
                av = _op(p, "vector", lambda e, rkv=rkv, rtv=rtv: e.tensor_copy(
                    rtv, rkv), inc=v_rt, drain=True)
            else:
                av = _op(p, "vector", lambda e, rkv=rkv, rtv=rtv: e.tensor_add(
                    rtv, rtv, rkv), inc=v_rt, drain=True)
            half_done[(k, h)] = av
    _wait(p, "gpsimd", g_rt, g_rt.n)
    _wait(p, "gpsimd", g_rt2, g_rt2.n)
    routed_done = v_rt.n

    _wait(p, "tensor", v_e, hsh_done_v)
    _wait(p, "tensor", c_e2, hsh_done_c)
    _wait(p, "tensor", s_ld, s_ld.n)
    comb_last = {}
    for ddc in range(8):
        for tq in range(4):
            bk = (ddc * 4 + tq) % 6
            if bk in comb_last:
                _wait(p, "tensor", v_rt, comb_last[bk])
            for fc in range(6):
                _op(p, "tensor", lambda e, ddc=ddc, tq=tq, fc=fc, bk=bk: e.matmul(
                    pb[bk][:, :], wsdn[:, fc, 128 * ddc:128 * (ddc + 1)],
                    hsh[:, fc, 512 * tq:512 * (tq + 1)],
                    start=(fc == 0), stop=False))
            tv = _op(p, "tensor", lambda e, ddc=ddc, tq=tq, bk=bk: e.matmul(
                pb[bk][:, :], y0row[:, 128 * ddc:128 * (ddc + 1)],
                ninv_row[:, 512 * tq:512 * (tq + 1)],
                start=False, stop=True), inc=t_e)
            _wait(p, "vector", t_e, tv)
            if ddc == 0 and tq == 0:
                _wait(p, "vector", v_rt, routed_done)
            cv = _op(p, "vector", lambda e, ddc=ddc, tq=tq, bk=bk: e.tensor_add(
                ffnT[:, ddc, 512 * tq:512 * (tq + 1)],
                routedT[:, ddc, 512 * tq:512 * (tq + 1)], pb[bk][:, :]), inc=v_rt)
            comb_last[bk] = cv
    ffnT_done = v_rt.n

    _wait(p, "sync", v_rt, ffnT_done)
    for c in range(8):
        _dma(p, "sync", s_tr, ffn_tm[:, :, 128 * c:128 * (c + 1)], ffnT[:, c, :],
             transpose=True)
    ftm_done = s_tr.n
    _wait(p, "vector", s_tr, ftm_done)
    for T in range(16):
        _op(p, "vector", lambda e, T=T: e.reduce_sum(
            stats[:, 0, T:T + 1], ffn_tm[:, T, :], axis=mybir.AxisListType.X))
        _op(p, "vector", lambda e, T=T: e.tensor_mul(
            dump[:], ffn_tm[:, T, :], ffn_tm[:, T, :]), drain=True)
        _op(p, "vector", lambda e, T=T: e.reduce_sum(
            stats[:, 1, T:T + 1], dump[:], axis=mybir.AxisListType.X), drain=True)
    _op(p, "vector", lambda e: e.drain())
    _op(p, "vector", lambda e: e.tensor_scalar_mul(stats[:, 2, :], stats[:, 0, :], 1.0 / D), drain=True)
    _op(p, "vector", lambda e: e.tensor_mul(stats[:, 3, :], stats[:, 2, :], stats[:, 2, :]), drain=True)
    vd = _op(p, "vector", lambda e: e.scalar_tensor_tensor(
        stats[:, 4, :], stats[:, 1, :], 1.0 / D, stats[:, 3, :],
        ALU.mult, ALU.subtract), inc=v_rt)
    _wait(p, "scalar", v_rt, vd)
    cd = _op(p, "scalar", lambda e: e.activation(
        out=dump[:, 0:16], in_=stats[:, 4, :], func=AF.Sqrt, bias=eps_sb[:]),
        inc=c_ev, drain=True)
    _wait(p, "vector", c_ev, cd)
    _op(p, "vector", lambda e: e.reciprocal(stats[:, 5, :], dump[:, 0:16]), drain=True)
    _wait(p, "sync", s_ld, s_ld.n)
    for qd in range(4):
        _wait(p, "sync", v_rt, v_rt.n)
        ld = _dma(p, "sync", s_ld, x1q[:, 0:4, :], x1_hbm[:, 4 * qd:4 * (qd + 1), :])
        _wait(p, "vector", s_ld, ld)
        for Ti in range(4):
            T = 4 * qd + Ti
            _op(p, "vector", lambda e, T=T: e.tensor_scalar(
                dump[:], ffn_tm[:, T, :], stats[:, 2, T:T + 1],
                stats[:, 5, T:T + 1], ALU.subtract, ALU.mult), drain=True)
            _op(p, "vector", lambda e, T=T, Ti=Ti: e.tensor_add(
                dump[:], x1q[:, Ti, :], dump[:]), drain=True)
            vv = _op(p, "vector", lambda e: _nop(e), inc=v_rt)
            _wait(p, "sync", v_rt, vv)
            _dma(p, "sync", s_st,
                 out_my.rearrange("(T pp) d -> pp T d", pp=128)[:, T, :], dump[:])
            _wait(p, "vector", s_st, s_st.n)
    _wait(p, "sync", s_st, s_st.n)
    _wait(p, "sync", s_g, s_g.n)
    _wait(p, "sync", g_d2, g_d2.n)
    _wait(p, "sync", g_rt, g_rt.n)
    _wait(p, "sync", g_ga, g_ga.n)
    if os.environ.get("MKDBG", "0") == "1":
        _wait(p, "sync", v_rt, v_rt.n)
        _wait(p, "sync", v_e, v_e.n)
        _wait(p, "sync", cc, cc.n)
        for nm, src in [("x1n_my", g("x1n_my")), ("a1_out", g("a1_out")),
                        ("a2_out", g("a2_out")), ("tok8_dram", g("tok8_dram")),
                        ("gate8_dram", g("gate8_dram")), ("ridx_dram", g("ridx_dram")),
                        ("ninv_dram", g("ninv_dram")), ("y_my", g("y_my")),
                        ("cnt_dram", g("cnt_dram"))]:
            dbg = nc.dram_tensor("dbg_" + nm, list(src.shape),
                                 src.dtype, kind="ExternalOutput").ap()
            _dma(p, "sync", s_st, dbg, src)
        _wait(p, "sync", s_st, s_st.n)
    return dict(locals())


def build_mega():
    nc, p, L = build_full()
    B = emit_bc(nc, p, L)
    if os.environ.get("MKBC", "full") != "full":
        lower_extended_insts(nc)
        return nc
    Cc = emit_c(nc, p, L, B)
    p.finalize()
    lower_extended_insts(nc)
    return nc


os.environ.setdefault("TRN_ALLOW_ICI_ALL_TO_ALL", "1")

_NC_CACHE = [None]

def kernel(**inputs):
    if _NC_CACHE[0] is None:
        _NC_CACHE[0] = build_mega()
    nc = _NC_CACHE[0]
    x = inputs["L0_attn_x"]
    base = {
        "Wq": inputs["L0_attn_Wq"], "Wk": inputs["L0_attn_Wk"],
        "Wv": inputs["L0_attn_Wv"], "Wo": inputs["L0_attn_Wo"],
        "bv": inputs["L0_attn_bv"],
        "bq_l": np.ascontiguousarray(inputs["L0_attn_bq"].reshape(8, 128).T),
        "bk_l": np.ascontiguousarray(inputs["L0_attn_bk"].reshape(8, 128).T),
        "bo_l": np.ascontiguousarray(inputs["L0_attn_bo"].reshape(8, 128).T),
        "router_w": inputs["L0_ffn_router_w"],
        "ws_gu": inputs["L0_ffn_ws_gate_up"],
        "ws_dn": inputs["L0_ffn_ws_down"],
    }
    in_maps = []
    for c in range(8):
        m = dict(base)
        m["x_my"] = np.ascontiguousarray(x[c])
        m["wg_my"] = np.ascontiguousarray(inputs["L0_ffn_w_gate"][8 * c:8 * c + 8])
        m["wu_my"] = np.ascontiguousarray(inputs["L0_ffn_w_up"][8 * c:8 * c + 8])
        m["wd_my"] = np.ascontiguousarray(inputs["L0_ffn_w_down"][8 * c:8 * c + 8])
        in_maps.append(m)
    res = run_bass_kernel_spmd(nc, in_maps, core_ids=list(range(8)))
    out = np.stack([np.asarray(res.results[c]["out_my"]) for c in range(8)])
    return out.astype(np.float32, copy=False)
